# version 01 (final)
# baseline (speedup 1.0000x reference)
import contextlib
import math

import numpy as np
import ml_dtypes

import concourse.bass as bass
import concourse.mybir as mybir
from concourse.bass_utils import run_bass_kernel_spmd

F32 = mybir.dt.float32
BF16 = mybir.dt.bfloat16
I32 = mybir.dt.int32
U8 = mybir.dt.uint8
AF = mybir.ActivationFunctionType
ALU = mybir.AluOpType
AX = mybir.AxisListType

DISABLE_AG = False

B, S, D = 4, 4096, 1024
NCORES = 8
T = 2048
NCH = D // 128
TC = 512
NTC = T // TC
CH = 128
NRCH = T // CH
H0 = 8
E0 = 128
H1 = 16
D1 = 64
FF0 = 4096
FF1 = 3072
GRP = 256
NGRP = T // GRP
ROWS = GRP * H1

EPS_OUT = 1e-5
EPS_A = 1e-6
EPS_F = 1e-6
EPS_1 = 1e-5
QSCALE = 1.0 / 127.0


ENGINES = ("sync", "gpsimd", "scalar", "vector", "tensor")
_DOM_OF_ENG = {"sync": "S", "gpsimd": "G", "scalar": "A", "vector": "V", "tensor": "T"}
_QDOM_OF_ENG = {"sync": "QS", "gpsimd": "QG", "scalar": "QA", "vector": "QV"}


class _Op:
    __slots__ = ("fn", "eng", "dom", "idx", "deps", "marked", "tag")

    def __init__(self, fn, eng, dom, idx, tag):
        self.fn = fn
        self.eng = eng
        self.dom = dom
        self.idx = idx
        self.deps = set()
        self.marked = False
        self.tag = tag


class Prog:

    def __init__(self):
        self.ops = {e: [] for e in ENGINES}
        self.by_dom = {}
        self.last_w = {}
        self.readers = {}
        self.ntags = {}

    def _emit(self, eng, dom, fn, reads, writes, tag):
        lst_dom = self.by_dom.setdefault(dom, [])
        op = _Op(fn, eng, dom, len(lst_dom), tag)
        ref = (dom, op.idx)
        deps = set()
        for k in reads:
            w = self.last_w.get(k)
            if w is not None:
                deps.add(w)
            self.readers.setdefault(k, []).append(ref)
        for k in writes:
            w = self.last_w.get(k)
            if w is not None:
                deps.add(w)
            for r in self.readers.get(k, ()):
                deps.add(r)
            self.last_w[k] = ref
            self.readers[k] = []
        my_compute_dom = _DOM_OF_ENG[eng]
        op.deps = {d for d in deps if d[0] != my_compute_dom}
        lst_dom.append(op)
        self.ops[eng].append(op)
        self.ntags[tag] = self.ntags.get(tag, 0) + 1
        return ref

    def comp(self, eng, fn, reads=(), writes=(), tag=""):
        return self._emit(eng, _DOM_OF_ENG[eng], fn, reads, writes, tag)

    def dma(self, eng, fn, reads=(), writes=(), tag="dma"):
        return self._emit(eng, _QDOM_OF_ENG[eng], fn, reads, writes, tag)

    def cc(self, fn, reads=(), writes=(), tag="cc"):
        return self._emit("gpsimd", "CC", fn, reads, writes, tag)

    def materialize(self, nc, stack):
        for dom, lst in self.by_dom.items():
            for op in lst:
                for (pd, pi) in op.deps:
                    self.by_dom[pd][pi].marked = True
        val = {}
        for dom, lst in self.by_dom.items():
            if dom in ("QS", "QG", "QA", "QV"):
                for op in lst:
                    op.marked = True
                    val[(dom, op.idx)] = 16 * (op.idx + 1)
            elif dom == "CC":
                for op in lst:
                    op.marked = True
                    val[(dom, op.idx)] = op.idx + 1
            else:
                c = 0
                for op in lst:
                    if op.marked:
                        c += 1
                    val[(dom, op.idx)] = c
        used_doms = set()
        for lst in self.by_dom.values():
            for op in lst:
                for d in op.deps:
                    used_doms.add(d[0])
        sems = {}
        for dom in sorted(used_doms):
            sems[dom] = stack.enter_context(nc.semaphore(f"sem_{dom}"))
        self._sems = sems
        self._val = val
        block = stack.enter_context(nc.Block())
        prog = self

        def run_engine(h, eng):
            watermark = {}
            dma_doms = ("QS", "QG", "QA", "QV")
            for op in prog.ops[eng]:
                need = {}
                for (pd, pi) in op.deps:
                    v = val[(pd, pi)]
                    if pd not in sems:
                        continue
                    if v > need.get(pd, 0):
                        need[pd] = v
                for pd, v in sorted(need.items()):
                    if v > watermark.get(pd, 0):
                        h.wait_ge(sems[pd], v)
                        watermark[pd] = v
                inst = op.fn(h)
                if op.dom not in sems:
                    continue
                if op.dom in dma_doms:
                    inst.then_inc(sems[op.dom], 16)
                elif op.dom == "CC":
                    inst.then_inc(sems[op.dom], 1)
                elif op.marked:
                    inst.then_inc(sems[op.dom], 1)

        @block.sync
        def _(h):
            run_engine(h, "sync")

        @block.gpsimd
        def _(h):
            run_engine(h, "gpsimd")

        @block.scalar
        def _(h):
            run_engine(h, "scalar")

        @block.vector
        def _(h):
            run_engine(h, "vector")

        @block.tensor
        def _(h):
            run_engine(h, "tensor")

    def stats(self):
        return {e: len(v) for e, v in self.ops.items()}



PGSZ = 2048


class Buf:
    __slots__ = ("arena", "off", "nbytes", "name")

    def __init__(self, arena, off, nbytes, name):
        self.arena = arena
        self.off = off
        self.nbytes = nbytes
        self.name = name

    def ap(self, dtype, cols, parts=128, p0=0):
        dsz = 2 if dtype == BF16 else (1 if dtype == U8 else 4)
        assert cols * dsz <= self.nbytes, (self.name, cols, dsz, self.nbytes)
        v = self.arena.t[:, self.off:self.off + cols * dsz].bitcast(dtype)
        if parts == 128 and p0 == 0:
            return v
        return v[p0:p0 + parts, :]

    def keys(self, frm=0, to=None):
        if to is None:
            to = self.nbytes
        lo = (self.off + frm) // PGSZ
        hi = (self.off + to + PGSZ - 1) // PGSZ
        return [f"pg{i}" for i in range(lo, hi)]

    def ckeys(self, c, width_bytes):
        return self.keys(c * width_bytes, (c + 1) * width_bytes)


class Arena:
    def __init__(self, nc, stack, nbytes):
        self.t = stack.enter_context(nc.sbuf_tensor("arena", [128, nbytes], U8))[:, :]
        self.nbytes = nbytes
        self.free_list = [(0, nbytes)]
        self.peak = 0
        self.used = 0

    def alloc(self, nbytes, name="buf", align=64):
        nbytes = (nbytes + align - 1) // align * align
        for i, (off, sz) in enumerate(self.free_list):
            aoff = (off + align - 1) // align * align
            if aoff + nbytes <= off + sz:
                pre = aoff - off
                post = off + sz - (aoff + nbytes)
                repl = []
                if pre:
                    repl.append((off, pre))
                if post:
                    repl.append((aoff + nbytes, post))
                self.free_list[i:i + 1] = repl
                self.used += nbytes
                self.peak = max(self.peak, self.nbytes - sum(s for _, s in self.free_list))
                return Buf(self, aoff, nbytes, name)
        raise MemoryError(f"sbuf arena OOM for {name}: {nbytes}B, "
                          f"free={sorted(self.free_list, key=lambda x: -x[1])[:4]}")

    def free(self, buf):
        self.used -= buf.nbytes
        self.free_list.append((buf.off, buf.nbytes))
        self.free_list.sort()
        merged = []
        for off, sz in self.free_list:
            if merged and merged[-1][0] + merged[-1][1] == off:
                merged[-1] = (merged[-1][0], merged[-1][1] + sz)
            else:
                merged.append((off, sz))
        self.free_list = [tuple(x) for x in merged]


class Psum:

    def __init__(self, nc, stack):
        self.p0 = stack.enter_context(nc.psum_tensor("pb0", [128, 2048], F32))[:, :]
        self.p1 = stack.enter_context(nc.psum_tensor("pb1", [128, 2048], F32))[:, :]

    def bank(self, i, banks=1):
        t = self.p0 if i < 4 else self.p1
        j = i % 4
        assert j + banks <= 4
        return t[:, j * 512:(j + banks) * 512]

    @staticmethod
    def key(i, banks=1):
        return [f"psum/{b}" for b in range(i, i + banks)]


def _cdiv(a, b):
    return (a + b - 1) // b



def _host_tables():
    f = np.float64
    e = E0
    half = e // 2
    base = (np.arange(0, e, 2, dtype=f) + 0.4 * e) / (1.4 * e)
    gammas = 1.0 - 2.0 ** (-5.0 - np.arange(H0, dtype=f))
    lam = gammas[:, None] * base[None, :] ** (1.0 / 512.0)
    lam2 = np.repeat(lam, 2, axis=1)
    inv_freq = 1.0 / 10000.0 ** (np.arange(half, dtype=f) / half)

    out = {}
    out["ident"] = np.eye(128, dtype=np.float32)
    bb, aa = np.meshgrid(np.arange(CH), np.arange(CH), indexing="ij")
    out["trilmask"] = (bb <= aa).astype(np.float32)
    out["lamC"] = (lam2 ** CH).T.astype(np.float32).copy()
    return out, lam2, inv_freq


def _core_tables(halfidx, lam2, inv_freq):
    f = np.float64
    p = np.arange(halfidx * T, (halfidx + 1) * T, dtype=f)
    a = np.arange(T) % CH
    sinus = p[:, None] * inv_freq[None, :]
    sin2 = np.repeat(np.sin(sinus), 2, axis=1)
    cos2 = np.repeat(np.cos(sinus), 2, axis=1)
    lam_a = lam2[None, :, :] ** a[:, None, None]
    lam_mb = lam2[None, :, :] ** (-a[:, None, None])

    def pack(rot, sc):
        x = (rot[:, None, :] * sc).astype(np.float32)
        return np.ascontiguousarray(x.reshape(NRCH, CH, H0 * E0))

    out = {}
    out["cosq"] = pack(cos2, lam_a)
    out["sinq"] = pack(sin2, lam_a)
    out["cosk"] = pack(cos2, lam_mb)
    out["sink"] = pack(sin2, lam_mb)
    if halfidx == 0:
        out["corr"] = np.zeros((128, NRCH * H0), np.float32)
        out["statemask"] = np.ones((128, 1), np.float32)
    else:
        c = np.empty((128, NRCH * H0), f)
        for j in range(NRCH):
            for h in range(H0):
                c[:, j * H0 + h] = lam2[h] ** (CH * j)
        out["corr"] = c.astype(np.float32)
        out["statemask"] = np.zeros((128, 1), np.float32)
    return out



class K:
    def __init__(self, nc, stack, dbg_phase=None):
        self.nc = nc
        self.stack = stack
        self.p = Prog()
        self.ar = Arena(nc, stack, 204 * 1024)
        self.ps = Psum(nc, stack)
        self.dram = {}
        self.dbg_phase = dbg_phase
        self.uid = 0
        self.rows = {}
        self.cols = {}
        self.row_data = []
        self.col_data = []
        self.constvals = {}

    def un(self, pfx="t"):
        self.uid += 1
        return f"{pfx}{self.uid}"

    def din(self, name, shape, dtype=F32):
        t = self.nc.dram_tensor(name, list(shape), dtype, kind="ExternalInput").ap()
        self.dram[name] = t
        return t

    def dout(self, name, shape, dtype=F32):
        t = self.nc.dram_tensor(name, list(shape), dtype, kind="ExternalOutput").ap()
        self.dram[name] = t
        return t

    def dtmp(self, name, shape, dtype=F32):
        t = self.nc.dram_tensor(name, list(shape), dtype).ap()
        self.dram[name] = t
        return t

    def reg_rows(self, name, arr):
        arr = np.asarray(arr, np.float32).reshape(-1, 128)
        idx = len(self.row_data)
        self.rows[name] = (idx, arr.shape[0])
        self.row_data.extend(arr)
        return idx

    def reg_cols(self, name, arr):
        arr = np.asarray(arr, np.float32).reshape(128, -1)
        idx = sum(a.shape[1] for a in self.col_data)
        self.cols[name] = (idx, arr.shape[1])
        self.col_data.append(arr)
        return idx

    def row_ap(self, name, i=0, n=1):
        idx, cnt = self.rows[name]
        assert i + n <= cnt
        return self.sb_rows.ap(F32, 128, parts=n, p0=idx + i)

    def row1(self, name, i=0):
        return self.row_ap(name, i, 1)

    def col_ap(self, name, i=0, n=1):
        idx, cnt = self.cols[name]
        assert i + n <= cnt
        v = self.sb_vecs.ap(F32, self._nvcols)
        return v[:, idx + i: idx + i + n]

    def ld(self, dst_ap, src_ap, writes, reads=(), eng="sync", tag="ld"):
        return self.p.dma(eng, lambda h: h.dma_start(out=dst_ap, in_=src_ap),
                          reads=reads, writes=writes, tag=tag)

    def st(self, dst_ap, src_ap, reads, writes=(), eng="gpsimd", tag="st"):
        return self.p.dma(eng, lambda h: h.dma_start(out=dst_ap, in_=src_ap),
                          reads=reads, writes=writes, tag=tag)

    def mm(self, out_ap, lhsT, rhs, start, stop, reads, writes, tag="mm"):
        return self.p.comp(
            "tensor",
            lambda h: h.matmul(out_ap, lhsT, rhs, start=start, stop=stop),
            reads=reads, writes=writes, tag=tag)

    def constf(self, v):
        v = float(v)
        if v not in self.constvals:
            self.constvals[v] = len(self.constvals)
        return v

    def act(self, out_ap, in_ap, func, reads, writes, bias=None, scale=None,
            accum=None, tag="act"):
        kw = {}
        reads = list(reads)
        if bias is not None:
            kw["bias"] = bias
            if isinstance(bias, float):
                self.constf(bias)
                reads.append("cvals")
        if scale is not None:
            kw["scale"] = scale
            if isinstance(scale, float):
                self.constf(scale)
                reads.append("cvals")
        if accum is not None:
            kw["accum_out"] = accum
        return self.p.comp(
            "scalar",
            lambda h: h.activation(out=out_ap, in_=in_ap, func=func, **kw),
            reads=reads, writes=writes, tag=tag)

    def vtt(self, out_ap, in0, in1, op, reads, writes, tag="tt"):
        return self.p.comp(
            "vector", lambda h: h.tensor_tensor(out=out_ap, in0=in0, in1=in1, op=op),
            reads=reads, writes=writes, tag=tag)

    def vts(self, out_ap, in0, s1, s2, op0, op1, reads, writes, accum=None, tag="ts"):
        if accum is not None:
            return self.p.comp(
                "vector",
                lambda h: h.tensor_scalar(out=out_ap, in0=in0, scalar1=s1,
                                          scalar2=s2, op0=op0, op1=op1,
                                          accum_out=accum),
                reads=reads, writes=writes, tag=tag)
        return self.p.comp(
            "vector",
            lambda h: h.tensor_scalar(out=out_ap, in0=in0, scalar1=s1, scalar2=s2,
                                      op0=op0, op1=op1),
            reads=reads, writes=writes, tag=tag)

    def vcopy(self, out_ap, in_ap, reads, writes, tag="vcp"):
        return self.p.comp("vector", lambda h: h.tensor_copy(out=out_ap, in_=in_ap),
                           reads=reads, writes=writes, tag=tag)

    def scopy(self, out_ap, in_ap, reads, writes, tag="scp"):
        return self.p.comp("scalar", lambda h: h.copy(out=out_ap, in_=in_ap),
                           reads=reads, writes=writes, tag=tag)

    def transp(self, out_psum, in_ap, reads, writes, tag="tr"):
        if in_ap.dtype == BF16:
            ident = self.sb_ident_bf.ap(BF16, 128)
            ik = self.k_identbf
        else:
            ident = self.sb_ident.ap(F32, 128)
            ik = self.k_ident
        return self.p.comp(
            "tensor", lambda h: h.transpose(out_psum, in_ap, ident),
            reads=list(reads) + list(ik), writes=writes, tag=tag)


    def load_consts(self):
        ar = self.ar
        self.sb_rows = ar.alloc(512, "rows")
        self.sb_vecs = ar.alloc(self._nvcols * 4, "vecs")
        self.sb_ident = ar.alloc(512, "ident")
        self.sb_tril = ar.alloc(256, "tril")
        self.sb_ones_bf = ar.alloc(256, "ones_bf")
        self.sb_cvals = ar.alloc(256, "cvals")
        self.sb_dcols = ar.alloc(1024, "dcols")
        self.sb_jsel = ar.alloc(1024 * 4, "jsel")
        self.sb_jhot = ar.alloc(256 * 4, "jhot")
        self.lrow64 = ar.alloc(256 * 4, "lrow64")
        self.k_jsel = self.sb_jsel.keys()
        self.k_jhot = self.sb_jhot.keys()
        self._ebcols = {}
        self._dcol_next = 0
        self.k_rows = self.sb_rows.keys()
        self.k_vecs = self.sb_vecs.keys()
        self.k_ident = self.sb_ident.keys()
        self.k_tril = self.sb_tril.keys()
        self.k_onesbf = self.sb_ones_bf.keys()
        self.k_cvals = self.sb_cvals.keys()
        self.ld(self.sb_rows.ap(F32, 128), self.dram["rows"][:, :],
                writes=self.k_rows, tag="const")
        self.ld(self.sb_vecs.ap(F32, self._nvcols), self.dram["vecs"][:, :],
                writes=self.k_vecs, tag="const")
        self.ld(self.sb_ident.ap(F32, 128), self.dram["ident"][:, :],
                writes=self.k_ident, tag="const")
        self.ld(self.sb_cvals.ap(F32, 64), self.dram["cvals"][:, :],
                writes=self.k_cvals, tag="const")
        self.ld(self.sb_jsel.ap(F32, 1024, parts=16), self.dram["jsel"][:, :],
                writes=self.k_jsel, tag="const")
        self.ld(self.sb_jhot.ap(F32, 256), self.dram["jhot"][:, :],
                writes=self.k_jhot, tag="const")
        self.sb_ident_bf = ar.alloc(256, "identbf")
        self.k_identbf = self.sb_ident_bf.keys()
        self.vcopy(self.sb_ident_bf.ap(BF16, 128), self.sb_ident.ap(F32, 128),
                   reads=self.k_ident, writes=self.k_identbf, tag="const")
        tril_f32 = self.ar.alloc(512, "trilf")
        self.ld(tril_f32.ap(F32, 128), self.dram["trilmask"][:, :],
                writes=tril_f32.keys(), tag="const")
        self.vcopy(self.sb_tril.ap(BF16, 128), tril_f32.ap(F32, 128),
                   reads=tril_f32.keys(), writes=self.k_tril, tag="const")
        self.p.comp("vector",
                    lambda h: h.memset(self.sb_ones_bf.ap(BF16, 128), 1.0),
                    writes=self.k_onesbf, tag="const")
        self.ar.free(tril_f32)

    def load_x(self, dst, dram_name="xT"):
        for c in range(NCH):
            self.ld(self.chunk(dst, c),
                    self.dram[dram_name][c * 128:(c + 1) * 128, :],
                    writes=dst.ckeys(c, T * 4), tag="ldx")

    @staticmethod
    def chunk(buf, c, dtype=F32, width=T):
        dsz = 2 if dtype == BF16 else 4
        n = buf.nbytes // dsz
        return buf.ap(dtype, n)[:, c * width:(c + 1) * width]

    @staticmethod
    def ck(buf, c, dtype=F32, width=T):
        dsz = 2 if dtype == BF16 else 4
        return buf.ckeys(c, width * dsz)

    def ln_fm(self, src, dst, eps, gcol=None, bcol=None, nrm_f32=None,
              src_dtype=F32, tag="ln"):
        p = self.p
        ar = self.ar
        sq = ar.alloc(T * 2, "lnsq")
        xbf = ar.alloc(T * 2, "lnxbf")
        rowbuf = ar.alloc(T * 4, "lnrows")
        rowbuf2 = ar.alloc(T * 4, "lnrows2")
        rowbuf3 = ar.alloc(T * 4, "lnrows3")
        rsum = rowbuf.ap(F32, T, parts=1, p0=0)
        rsq = rowbuf2.ap(F32, T, parts=1, p0=0)
        rtmp = rowbuf3.ap(F32, T, parts=1, p0=0)
        krsum, krsq, krtmp = rowbuf.keys(), rowbuf2.keys(), rowbuf3.keys()
        ones_col_bf = self.sb_ones_bf.ap(BF16, 128)[:, 0:1]
        ones_row = self.row1("r_ones")

        sum_ps = self.ps.bank(4, 4)
        sq_ps = self.ps.bank(0, 4)
        sum_keys = Psum.key(4, 4)
        sq_keys = Psum.key(0, 4)
        for c in range(NCH):
            sc = self.chunk(src, c, src_dtype)
            self.vcopy(xbf.ap(BF16, T), sc, reads=self.ck(src, c, src_dtype),
                       writes=xbf.keys(), tag=tag + "c")
            for t in range(NTC):
                st = sum_ps[0:1, t * TC:(t + 1) * TC]
                self.mm(st, ones_col_bf, xbf.ap(BF16, T)[:, t * TC:(t + 1) * TC],
                        start=(c == 0), stop=(c == NCH - 1),
                        reads=xbf.keys() + self.k_onesbf,
                        writes=[sum_keys[t]], tag=tag + "s")
        self.scopy(rsum, sum_ps[0:1, :], reads=sum_keys, writes=krsum,
                   tag=tag + "e")
        for c in range(NCH):
            sc = self.chunk(src, c, src_dtype)
            self.act(sq.ap(BF16, T), sc, AF.Square,
                     reads=self.ck(src, c, src_dtype),
                     writes=sq.keys(), tag=tag + "q")
            for t in range(NTC):
                st = sq_ps[0:1, t * TC:(t + 1) * TC]
                self.mm(st, ones_col_bf, sq.ap(BF16, T)[:, t * TC:(t + 1) * TC],
                        start=(c == 0), stop=(c == NCH - 1),
                        reads=sq.keys() + self.k_onesbf,
                        writes=[sq_keys[t]], tag=tag + "s")
        self.scopy(rsq, sq_ps[0:1, :], reads=sq_keys, writes=krsq,
                   tag=tag + "e")
        self.p.comp("scalar", lambda h: h.mul(out=rsum, in_=rsum, mul=1.0 / D),
                    reads=krsum, writes=krsum, tag=tag + "r")
        self.p.comp("scalar", lambda h: h.mul(out=rsq, in_=rsq, mul=1.0 / D),
                    reads=krsq, writes=krsq, tag=tag + "r")
        self.p.comp("vector",
                    lambda h: h.scalar_tensor_tensor(
                        out=rtmp, in0=rsum, scalar=-1.0, in1=rsum,
                        op0=ALU.mult, op1=ALU.mult),
                    reads=krsum, writes=krtmp, tag=tag + "r")
        self.vtt(rsq, rsq, rtmp, ALU.add, reads=krsq + krtmp,
                 writes=krsq, tag=tag + "r")
        self.act(rsq, rsq, AF.Ln, reads=krsq, writes=krsq,
                 bias=float(eps), scale=1.0, tag=tag + "r")
        self.act(rsq, rsq, AF.Exp, reads=krsq, writes=krsq,
                 scale=-0.5, tag=tag + "r")
        self.p.comp("vector",
                    lambda h: h.scalar_tensor_tensor(
                        out=rsum, in0=rsum, scalar=-1.0, in1=rsq,
                        op0=ALU.mult, op1=ALU.mult),
                    reads=krsum + krsq, writes=krsum, tag=tag + "r")
        ub = self.ps.bank(0, 4)
        db = self.ps.bank(4, 4)
        for t in range(NTC):
            self.mm(ub[:, t * TC:(t + 1) * TC], ones_row,
                    rsq[:, t * TC:(t + 1) * TC], start=True, stop=True,
                    reads=krsq + self.k_rows, writes=[Psum.key(0 + t)[0]],
                    tag=tag + "b")
            self.mm(db[:, t * TC:(t + 1) * TC], ones_row,
                    rsum[:, t * TC:(t + 1) * TC], start=True, stop=True,
                    reads=krsum + self.k_rows, writes=[Psum.key(4 + t)[0]],
                    tag=tag + "b")
        ukeys = Psum.key(0, 4)
        dkeys = Psum.key(4, 4)
        t1 = ar.alloc(T * 4, "lnt1")
        for c in range(NCH):
            sc = self.chunk(src, c, src_dtype)
            self.vtt(t1.ap(F32, T), sc, ub, ALU.mult,
                     reads=self.ck(src, c, src_dtype) + ukeys, writes=t1.keys(),
                     tag=tag + "a")
            if nrm_f32 is not None:
                tgt, tk = self.chunk(nrm_f32, c), self.ck(nrm_f32, c)
                self.vtt(tgt, t1.ap(F32, T), db, ALU.add,
                         reads=t1.keys() + dkeys, writes=tk, tag=tag + "a")
                srcn, srck = tgt, tk
            else:
                self.vtt(t1.ap(F32, T), t1.ap(F32, T), db, ALU.add,
                         reads=t1.keys() + dkeys, writes=t1.keys(), tag=tag + "a")
                srcn, srck = t1.ap(F32, T), t1.keys()
            if dst is not None:
                dstc = self.chunk(dst, c, BF16)
                dk = self.ck(dst, c, BF16)
                if gcol is not None:
                    self.vts(dstc, srcn, gcol(c), bcol(c), ALU.mult, ALU.add,
                             reads=srck + self.k_vecs, writes=dk, tag=tag + "a")
                else:
                    self.vcopy(dstc, srcn, reads=srck, writes=dk, tag=tag + "a")
        self.ar.free(sq)
        self.ar.free(xbf)
        self.ar.free(rowbuf)
        self.ar.free(rowbuf2)
        self.ar.free(rowbuf3)
        self.ar.free(t1)

    def init_wstage(self):
        self.wbf = [self.ar.alloc(NCH * 1024 * 2, f"wbf{i}") for i in range(2)]
        self.wpar = 0

    def load_weight(self, dram_slices, tag="ldw"):
        buf = self.wbf[self.wpar]
        self.wpar ^= 1
        dout = dram_slices[0].shape[-1]
        for dc in range(NCH):
            dst = buf.ap(BF16, NCH * 1024)[:, dc * 1024: dc * 1024 + dout]
            self.p.dma("gpsimd",
                       lambda h, d=dst, s=dram_slices[dc]: h.dma_start(out=d, in_=s),
                       reads=(), writes=buf.ckeys(dc, 1024 * 2), tag=tag)
        return buf, dout

    def wslice(self, buf, dc, o0, o1):
        return buf.ap(BF16, NCH * 1024)[:, dc * 1024 + o0: dc * 1024 + o1]

    def proj_fm(self, src, wbuf, dst, dout=D, func=None, bias=None, scale=None,
                accum_from=None, dst_dtype=BF16, dst_width=None, dst_off=0,
                pb=(0, 1), tag="proj"):
        noc = dout // 128
        dw = dst_width or T
        for oc in range(noc):
            for tc in range(NTC):
                bank = pb[(oc * NTC + tc) % 2]
                ps = self.ps.bank(bank)
                pk = Psum.key(bank)
                for dc in range(NCH):
                    self.mm(ps, self.wslice(wbuf, dc, oc * 128, (oc + 1) * 128),
                            self.chunk(src, dc, BF16)[:, tc * TC:(tc + 1) * TC],
                            start=(dc == 0), stop=(dc == NCH - 1),
                            reads=wbuf.ckeys(dc, 1024 * 2) + self.ck(src, dc, BF16),
                            writes=pk, tag=tag + "m")
                oco = oc + dst_off
                dstc = self.chunk(dst, oco, dst_dtype, dw)[:, tc * TC:(tc + 1) * TC]
                dk = self.ck(dst, oco, dst_dtype, dw)
                kw = {}
                rd = list(pk)
                if bias is not None:
                    kw["bias"] = bias(oc)
                    rd += self.k_vecs
                if scale is not None:
                    kw["scale"] = scale(oc)
                    rd += self.k_vecs
                self.act(dstc, ps, func or AF.Identity, reads=rd, writes=dk,
                         tag=tag + "e", **kw)

    def load_fm(self, dst, dram_ap, dtype=BF16, tag="ldn"):
        for c in range(NCH):
            self.ld(self.chunk(dst, c, dtype), dram_ap[c * 128:(c + 1) * 128, :],
                    writes=self.ck(dst, c, dtype), tag=tag)

    def proj_fm_stream(self, src, wbuf, dram_ap, dkey, dout=D, func=None,
                       bias=None, scale=None, stt_addbuf=None, pb=(0, 1),
                       tag="projs"):
        noc = dout // 128
        stage = self.ar.alloc(TC * 4, "pstage")
        sdt = BF16 if dram_ap.dtype == BF16 else F32
        for oc in range(noc):
            for tc in range(NTC):
                bank = pb[(oc * NTC + tc) % 2]
                ps = self.ps.bank(bank)
                pk = Psum.key(bank)
                for dc in range(NCH):
                    self.mm(ps, self.wslice(wbuf, dc, oc * 128, (oc + 1) * 128),
                            self.chunk(src, dc, BF16)[:, tc * TC:(tc + 1) * TC],
                            start=(dc == 0), stop=(dc == NCH - 1),
                            reads=wbuf.ckeys(dc, 1024 * 2) + self.ck(src, dc, BF16),
                            writes=pk, tag=tag + "m")
                sg = stage.ap(sdt, TC)
                if stt_addbuf is not None:
                    abuf, scol = stt_addbuf
                    addap = self.chunk(abuf, oc, BF16)[:, tc * TC:(tc + 1) * TC]
                    self.p.comp(
                        "vector",
                        lambda h, o=sg, i=ps, s=scol(oc), a=addap:
                            h.scalar_tensor_tensor(out=o, in0=i, scalar=s, in1=a,
                                                   op0=ALU.mult, op1=ALU.add),
                        reads=pk + self.k_vecs + self.ck(abuf, oc, BF16),
                        writes=stage.keys(), tag=tag + "e")
                else:
                    kw = {}
                    rd = list(pk)
                    if bias is not None:
                        kw["bias"] = bias(oc)
                        rd += self.k_vecs
                    if scale is not None:
                        kw["scale"] = scale(oc)
                        rd += self.k_vecs
                    self.act(sg, ps, func or AF.Identity, reads=rd,
                             writes=stage.keys(), tag=tag + "e", **kw)
                self.st(dram_ap[oc * 128:(oc + 1) * 128, tc * TC:(tc + 1) * TC],
                        sg, reads=stage.keys(), writes=[f"{dkey}/{oc}/{tc}"],
                        tag=tag + "d")
        self.ar.free(stage)

    def proj_tm_stream(self, src, wbuf, dram_ap, dkey, dout=D, tag="projt"):
        stage = self.ar.alloc(1024 * 4, "tstage")
        nhalf = dout // TC
        for tt in range(T // 128):
            ps = self.ps.bank(0 if tt % 2 == 0 else 2, 2)
            pk = Psum.key(0 if tt % 2 == 0 else 2, 2)
            for dc in range(NCH):
                for g in range(nhalf):
                    self.mm(ps[:, g * TC:(g + 1) * TC],
                            self.chunk(src, dc, BF16)[:, tt * 128:(tt + 1) * 128],
                            self.wslice(wbuf, dc, g * TC, (g + 1) * TC),
                            start=(dc == 0), stop=(dc == NCH - 1),
                            reads=wbuf.ckeys(dc, 1024 * 2) + self.ck(src, dc, BF16),
                            writes=[pk[g]], tag=tag + "m")
            sg = stage.ap(BF16, 1024)
            self.scopy(sg[:, 0:dout], ps[:, 0:dout], reads=pk,
                       writes=stage.keys(), tag=tag + "e")
            self.st(dram_ap[tt * 128:(tt + 1) * 128, :], sg[:, 0:dout],
                    reads=stage.keys(), writes=[f"{dkey}/{tt}"], tag=tag + "d")
        self.ar.free(stage)

    def l0ffn(self, dbg=None):
        ar = self.ar
        edges = []
        lind = 0
        for c in range(6):
            nsrc = min(c + 1, 5)
            eq, ek, ev = lind + nsrc - 1, lind, lind + nsrc // 2
            in_q, in_k, in_v = c - 1, c - nsrc, c - nsrc + nsrc // 2
            edges.append((eq, ek, ev, in_q, in_k, in_v))
            lind += nsrc

        nd = self.dram["ndbf"]
        qcur = ar.alloc(NCH * T * 2, "qcur")
        qh = ar.alloc(NCH * T * 2, "qh")

        def eb_col(e, j):
            return self._eb(e, j)

        loaded = {"cur": None}
        slots = {}

        def load_node(i):
            srcslot = slots["src"]
            if loaded["cur"] != i:
                for c in range(NCH):
                    self.ld(self.chunk(srcslot, c, BF16),
                            nd[i + 1, c * 128:(c + 1) * 128, :],
                            reads=([f"nd0/{c}"] if i == -1 else
                                   [f"nd{i + 1}/{oc}/{tc}" for oc in range(NCH)
                                    for tc in range(NTC)]),
                            writes=self.ck(srcslot, c, BF16), eng="gpsimd",
                            tag="ndl")
                loaded["cur"] = i

        for c in range(6):
            eq, ek, ev, in_q, in_k, in_v = edges[c]
            srcslot = ar.alloc(NCH * T * 2, "srcslot")
            edgeslot = ar.alloc(NCH * T * 2, "edgeslot")
            slots["src"] = srcslot
            loaded["cur"] = None
            ewc = lambda j: (lambda oc: self.col_ap("ew", c * 3 + j))
            load_node(in_k)
            wb, _ = self.load_weight([self.dram["fWe"][ek, dc * 128:(dc + 1) * 128, :]
                                      for dc in range(NCH)])
            self.proj_fm(srcslot, wb, edgeslot, func=AF.Identity,
                         bias=self._ebcol(c, 1, ek), scale=ewc(1), tag="edgek")
            wb, _ = self.load_weight([self.dram["fWk"][c, dc * 128:(dc + 1) * 128, :]
                                      for dc in range(NCH)])
            self.proj_fm_stream(edgeslot, wb, self.dram["ag_in"][0:D, :],
                                f"agki{c}", tag="khs")
            if (c, "v") == (0, "v") and ev == ek and in_v == in_k:
                pass
            load_node(in_v)
            wb, _ = self.load_weight([self.dram["fWe"][ev, dc * 128:(dc + 1) * 128, :]
                                      for dc in range(NCH)])
            self.proj_fm(srcslot, wb, edgeslot, func=AF.Identity,
                         bias=self._ebcol(c, 2, ev), scale=ewc(2), tag="edgev")
            wb, _ = self.load_weight([self.dram["fWv"][c, dc * 128:(dc + 1) * 128, :]
                                      for dc in range(NCH)])
            vh_view = self.dram["ag_in"][D:T, :].rearrange("p (a c) -> (p a) c", a=2)
            self.proj_tm_stream(edgeslot, wb, vh_view, f"agvi{c}", tag="vhs")
            agk_keys = [f"agki{c}/{oc}/{tc}" for oc in range(NCH) for tc in range(NTC)]
            agv_keys = [f"agvi{c}/{tt}" for tt in range(T // 128)]
            if DISABLE_AG:
                self.st(self.dram["ag_out"][0:T, :], self.dram["ag_in"][:, :],
                        reads=agk_keys + agv_keys, writes=["ag_out"], tag="agf")
                self.st(self.dram["ag_out"][T:2 * T, :], self.dram["ag_in"][:, :],
                        reads=agk_keys + agv_keys, writes=["ag_out"], tag="agf")
            else:
                self.p.cc(lambda h: h.collective_compute(
                              "AllGather", ALU.bypass,
                              replica_groups=[[0, 1], [2, 3], [4, 5], [6, 7]],
                              ins=[self.dram["ag_in"][:, :]],
                              outs=[self.dram["ag_out"][:, :]]),
                          reads=agk_keys + agv_keys,
                          writes=["ag_out"], tag="ag")
            load_node(in_q)
            wb, _ = self.load_weight([self.dram["fWe"][eq, dc * 128:(dc + 1) * 128, :]
                                      for dc in range(NCH)])
            self.proj_fm(srcslot, wb, qcur, func=AF.Identity,
                         bias=self._ebcol(c, 0, eq), scale=ewc(0), tag="edgeq")
            wb, _ = self.load_weight([self.dram["fWq"][c, dc * 128:(dc + 1) * 128, :]
                                      for dc in range(NCH)])
            self.proj_fm(qcur, wb, qh, tag="qh")
            ar.free(srcslot)
            ar.free(edgeslot)
            if dbg == f"fq{c}":
                self.dump_fm(qh)
                return True
            ctx = ar.alloc(NCH * T * 2, "ctx")
            self._mha_l0(qh, ctx, tag=f"mha{c}")
            wb, _ = self.load_weight([self.dram["fWo"][c, dc * 128:(dc + 1) * 128, :]
                                      for dc in range(NCH)])
            if c == 5:
                self.proj_fm_stream(ctx, wb, self.dram["n5f"], "n5f",
                                    stt_addbuf=(qcur, lambda oc: self.col_ap("act_w", c)),
                                    tag="wo5")
            else:
                self.proj_fm_stream(ctx, wb, nd[c + 1], f"nd{c + 1}",
                                    stt_addbuf=(qcur, lambda oc: self.col_ap("act_w", c)),
                                    tag="wo")
            self.ar.free(ctx)
            if dbg == f"fnode{c}":
                dsrc = self.dram["n5f"] if c == 5 else nd[c + 1]
                dt = F32
                tmp = ar.alloc(NCH * T * 4, "dbgld")
                rk = ["n5f/%d/%d" % (oc, tc) for oc in range(NCH) for tc in range(NTC)] \
                    if c == 5 else [f"nd{c + 1}/{oc}/{tc}" for oc in range(NCH)
                                    for tc in range(NTC)]
                for cc in range(NCH):
                    self.ld(self.chunk(tmp, cc, dt), dsrc[cc * 128:(cc + 1) * 128, :],
                            reads=rk, writes=self.ck(tmp, cc, dt), tag="dbgl")
                self.dump_fm(tmp, dt)
                self.ar.free(tmp)
                return True
        ar.free(qcur)
        ar.free(qh)
        n5 = ar.alloc(NCH * T * 4, "n5")
        self.load_fm(n5, self.dram["n5f"], F32, tag="n5l")
        g0 = ar.alloc(NCH * T * 2, "g0")
        self.ln_fm(n5, g0, EPS_F,
                   gcol=lambda c: self.col_ap("f_ln_g", c),
                   bcol=lambda c: self.col_ap("f_ln_b", c), tag="lnf")
        ar.free(n5)
        if dbg == "fln":
            self.dump_fm(g0)
            return True
        for c in range(NCH):
            self.act(self.chunk(g0, c, BF16), self.chunk(g0, c, BF16), AF.Silu,
                     reads=self.ck(g0, c, BF16), writes=self.ck(g0, c, BF16),
                     tag="silg")
        if dbg == "fsil":
            self.dump_fm(g0)
            return True
        stage = ar.alloc(T * 2, "gst")
        for c in range(NCH):
            self.st(self.dram["g0_d"][c * 128:(c + 1) * 128, :],
                    self.chunk(g0, c, BF16), reads=self.ck(g0, c, BF16),
                    writes=[f"g0_d/{c}"], tag="g0st")
        ar.free(stage)
        ar.free(g0)
        return False

    def _ebcol(self, c, j, e):
        key = (c, j, e)
        if key not in self._ebcols:
            buf = self.sb_dcols
            idx = self._dcol_next
            self._dcol_next += 8
            assert self._dcol_next <= 256
            dst = buf.ap(F32, 256)[:, idx:idx + 8]
            self.vts(dst, self.col_ap(f"f_be{e}", 0, 8),
                     self.col_ap("ew", c * 3 + j), None, ALU.mult, ALU.bypass,
                     reads=self.k_vecs, writes=buf.keys(), tag="ebprep")
            self._ebcols[key] = idx
        idx = self._ebcols[key]
        return lambda oc: self.sb_dcols.ap(F32, 256)[:, idx + oc:idx + oc + 1]

    def _mha_l0(self, qh, ctx, tag="mha"):
        ar = self.ar
        S2 = 2 * T
        khs = ar.alloc(S2 * 2, "khs")
        vhs = ar.alloc(S2 * 2, "vhs")
        pts = ar.alloc(1024 * 2 * 2, "pts")
        lrow = ar.alloc(T * 4, "lrow")
        rrow = ar.alloc(T * 4, "rrow")
        rb = ar.alloc(1024 * 4, "rb")
        sc = float(1.0 / math.sqrt(E0))
        ones_col_bf = self.sb_ones_bf.ap(BF16, 128)[:, 0:1]
        ones_row = self.row1("r_ones")
        for h in range(H0):
            kap = khs.ap(BF16, S2)
            ago = self.dram["ag_out"]
            self.ld(kap[:, 0:T], ago[h * 128:(h + 1) * 128, :],
                    writes=khs.keys(), reads=["ag_out"], tag=tag + "lk")
            self.ld(kap[:, T:S2], ago[T + h * 128:T + (h + 1) * 128, :],
                    writes=khs.keys(), reads=["ag_out"], tag=tag + "lk")
            vap = vhs.ap(BF16, S2)
            for s in range(2):
                vview = ago[s * T + D:(s + 1) * T, :].rearrange(
                    "p (a c) -> (p a) c", a=2)
                for kt in range(16):
                    self.ld(vap[:, (s * 16 + kt) * 128:(s * 16 + kt + 1) * 128],
                            vview[kt * 128:(kt + 1) * 128, h * 128:(h + 1) * 128],
                            writes=vhs.keys(), reads=["ag_out"], tag=tag + "lv")
            for qc in range(2):
                ctx_ps = self.ps.bank(4, 2)
                ctx_k = Psum.key(4, 2)
                l_ps = self.ps.bank(6, 2)
                l_k = Psum.key(6, 2)
                qsl = self.chunk(qh, h, BF16)[:, qc * 1024:(qc + 1) * 1024]
                qk = self.ck(qh, h, BF16)
                for kt in range(32):
                    pb = kt % 2
                    st_ps = self.ps.bank(0 + 2 * pb, 2)
                    st_k = Psum.key(0 + 2 * pb, 2)
                    for j in range(2):
                        self.mm(st_ps[:, j * TC:(j + 1) * TC],
                                kap[:, kt * 128:(kt + 1) * 128],
                                qsl[:, j * TC:(j + 1) * TC],
                                start=True, stop=True,
                                reads=khs.keys() + qk, writes=[st_k[j]],
                                tag=tag + "s")
                    pt = pts.ap(BF16, 2048)[:, pb * 1024:(pb + 1) * 1024]
                    self.act(pt, st_ps, AF.Exp, reads=st_k, writes=pts.keys(),
                             scale=sc, tag=tag + "x")
                    for j in range(2):
                        self.mm(ctx_ps[:, j * TC:(j + 1) * TC],
                                vap[:, kt * 128:(kt + 1) * 128],
                                pt[:, j * TC:(j + 1) * TC],
                                start=(kt == 0), stop=(kt == 31),
                                reads=vhs.keys() + pts.keys(), writes=[ctx_k[j]],
                                tag=tag + "pv")
                        self.mm(l_ps[0:1, j * TC:(j + 1) * TC], ones_col_bf,
                                pt[:, j * TC:(j + 1) * TC],
                                start=(kt == 0), stop=(kt == 31),
                                reads=self.k_onesbf + pts.keys(), writes=[l_k[j]],
                                tag=tag + "l")
                lr = lrow.ap(F32, T, parts=1)[:, 0:1024]
                rr = rrow.ap(F32, T, parts=1)[:, 0:1024]
                self.scopy(lr, l_ps[0:1, :1024], reads=l_k, writes=lrow.keys(),
                           tag=tag + "le")
                self.p.comp("vector", lambda h2, o=rr, i=lr: h2.reciprocal(out=o, in_=i),
                            reads=lrow.keys(), writes=rrow.keys(), tag=tag + "r")
                rbp = self.ps.bank(6, 2)
                rb_k = Psum.key(6, 2)
                for j in range(2):
                    self.mm(rbp[:, j * TC:(j + 1) * TC], ones_row,
                            rr[:, j * TC:(j + 1) * TC], start=True, stop=True,
                            reads=rrow.keys() + self.k_rows, writes=rb_k,
                            tag=tag + "rb")
                self.scopy(rb.ap(F32, 1024), rbp, reads=rb_k, writes=rb.keys(),
                           tag=tag + "rbe")
                csl = self.chunk(ctx, h, BF16)[:, qc * 1024:(qc + 1) * 1024]
                self.vtt(csl, ctx_ps, rb.ap(F32, 1024), ALU.mult,
                         reads=ctx_k + rb.keys(), writes=self.ck(ctx, h, BF16),
                         tag=tag + "dv")
        ar.free(khs)
        ar.free(vhs)
        ar.free(pts)
        ar.free(lrow)
        ar.free(rrow)
        ar.free(rb)

    def gn_fm(self, src_dram, src_keys, dst, gcol, bcol, eps, mul_into=None,
              tag="gn"):
        ar = self.ar
        cb = ar.alloc(T * 2, "gncb")
        sq = ar.alloc(T * 2, "gnsq")
        rowbuf = ar.alloc(T * 4, "gnr1")
        rowbuf2 = ar.alloc(T * 4, "gnr2")
        rowbuf3 = ar.alloc(T * 4, "gnr3")
        rsum = rowbuf.ap(F32, T, parts=1, p0=0)
        rsq = rowbuf2.ap(F32, T, parts=1, p0=0)
        rtmp = rowbuf3.ap(F32, T, parts=1, p0=0)
        krsum, krsq, krtmp = rowbuf.keys(), rowbuf2.keys(), rowbuf3.keys()
        ones_col_bf = self.sb_ones_bf.ap(BF16, 128)[:, 0:1]
        ones_row = self.row1("r_ones")
        E = 128.0
        for c in range(NCH):
            u = self.un(tag)
            self.ld(cb.ap(BF16, T), src_dram[c * 128:(c + 1) * 128, :],
                    reads=src_keys, writes=cb.keys(), tag=tag + "l")
            self.act(sq.ap(BF16, T), cb.ap(BF16, T), AF.Square, reads=cb.keys(),
                     writes=sq.keys(), tag=tag + "q")
            sps = self.ps.bank(4, 4)
            qps = self.ps.bank(0, 4)
            skeys = Psum.key(4, 4)
            qkeys = Psum.key(0, 4)
            for t in range(NTC):
                self.mm(sps[0:1, t * TC:(t + 1) * TC], ones_col_bf,
                        cb.ap(BF16, T)[:, t * TC:(t + 1) * TC], start=True,
                        stop=True, reads=cb.keys() + self.k_onesbf,
                        writes=[skeys[t]], tag=tag + "s")
                self.mm(qps[0:1, t * TC:(t + 1) * TC], ones_col_bf,
                        sq.ap(BF16, T)[:, t * TC:(t + 1) * TC], start=True,
                        stop=True, reads=sq.keys() + self.k_onesbf,
                        writes=[qkeys[t]], tag=tag + "s")
            self.scopy(rsum, sps[0:1, :], reads=skeys, writes=krsum, tag=tag + "e")
            self.scopy(rsq, qps[0:1, :], reads=qkeys, writes=krsq, tag=tag + "e")
            self.p.comp("scalar", lambda h, o=rsum: h.mul(out=o, in_=o, mul=1.0 / E),
                        reads=krsum, writes=krsum, tag=tag + "r")
            self.p.comp("scalar", lambda h, o=rsq: h.mul(out=o, in_=o, mul=1.0 / E),
                        reads=krsq, writes=krsq, tag=tag + "r")
            self.p.comp("vector",
                        lambda h, o=rtmp, i=rsum: h.scalar_tensor_tensor(
                            out=o, in0=i, scalar=-1.0, in1=i,
                            op0=ALU.mult, op1=ALU.mult),
                        reads=krsum, writes=krtmp, tag=tag + "r")
            self.vtt(rsq, rsq, rtmp, ALU.add, reads=krsq + krtmp, writes=krsq,
                     tag=tag + "r")
            self.act(rsq, rsq, AF.Ln, reads=krsq, writes=krsq, bias=float(eps),
                     scale=1.0, tag=tag + "r")
            self.act(rsq, rsq, AF.Exp, reads=krsq, writes=krsq, scale=-0.5,
                     tag=tag + "r")
            self.p.comp("vector",
                        lambda h, o=rsum, i=rsq: h.scalar_tensor_tensor(
                            out=o, in0=o, scalar=-1.0, in1=i,
                            op0=ALU.mult, op1=ALU.mult),
                        reads=krsum + krsq, writes=krsum, tag=tag + "r")
            ub = self.ps.bank(0, 4)
            db = self.ps.bank(4, 4)
            for t in range(NTC):
                self.mm(ub[:, t * TC:(t + 1) * TC], ones_row,
                        rsq[:, t * TC:(t + 1) * TC], start=True, stop=True,
                        reads=krsq + self.k_rows, writes=[qkeys[t]], tag=tag + "b")
                self.mm(db[:, t * TC:(t + 1) * TC], ones_row,
                        rsum[:, t * TC:(t + 1) * TC], start=True, stop=True,
                        reads=krsum + self.k_rows, writes=[skeys[t]], tag=tag + "b")
            t1b = self.ar.alloc(T * 4, "gnt1")
            self.vtt(t1b.ap(F32, T), cb.ap(BF16, T), ub, ALU.mult,
                     reads=cb.keys() + qkeys, writes=t1b.keys(), tag=tag + "a")
            self.vtt(t1b.ap(F32, T), t1b.ap(F32, T), db, ALU.add,
                     reads=t1b.keys() + skeys, writes=t1b.keys(), tag=tag + "a")
            if mul_into is None:
                self.vts(self.chunk(dst, c, BF16), t1b.ap(F32, T), gcol(c),
                         bcol(c), ALU.mult, ALU.add,
                         reads=t1b.keys() + self.k_vecs,
                         writes=self.ck(dst, c, BF16), tag=tag + "a")
            else:
                ych = self.ar.alloc(T * 2, "gny1")
                self.vts(ych.ap(BF16, T), t1b.ap(F32, T), gcol(c), bcol(c),
                         ALU.mult, ALU.add, reads=t1b.keys() + self.k_vecs,
                         writes=ych.keys(), tag=tag + "a")
                mc = self.chunk(mul_into, c, BF16)
                self.vtt(mc, mc, ych.ap(BF16, T), ALU.mult,
                         reads=self.ck(mul_into, c, BF16) + ych.keys(),
                         writes=self.ck(mul_into, c, BF16), tag=tag + "a")
                self.ar.free(ych)
            self.ar.free(t1b)
        ar.free(cb)
        ar.free(sq)
        ar.free(rowbuf)
        ar.free(rowbuf2)
        ar.free(rowbuf3)

    def l0attn(self, z, dbg=None):
        ar = self.ar
        for l in range(2):
            a = ar.alloc(NCH * T * 2, "ret_a")
            self.ln_fm(z, a, EPS_A,
                       gcol=lambda c: self.col_ap(f"a_ln1_s{l}", c),
                       bcol=lambda c: self.col_ap(f"a_ln1_b{l}", c), tag=f"aln{l}")
            self._retention(a, l, dbg)
            if dbg == f"ret{l}":
                tmp = ar.alloc(NCH * T * 2, "dbgh")
                for cc in range(NCH):
                    self.ld(self.chunk(tmp, cc, BF16),
                            self.dram["heads_d"][cc * 128:(cc + 1) * 128, :],
                            reads=[f"hd{l}/{j}/{h}" for j in range(NRCH)
                                   for h in range(H0)],
                            writes=self.ck(tmp, cc, BF16), tag="dbgh")
                self.dump_fm(tmp)
                return True
            gate = ar.alloc(NCH * T * 2, "gate")
            wb, _ = self.load_weight([self.dram["aWg"][l, dc * 128:(dc + 1) * 128, :]
                                      for dc in range(NCH)])
            self.proj_fm(a, wb, gate, func=AF.Silu, tag=f"wg{l}")
            ar.free(a)
            self.gn_fm(self.dram["heads_d"],
                       [f"hd{l}/{j}/{h}" for j in range(NRCH) for h in range(H0)],
                       None, lambda c: self.col_ap(f"a_gn_s{l}", c),
                       lambda c: self.col_ap(f"a_gn_b{l}", c), EPS_A,
                       mul_into=gate, tag=f"gn{l}")
            if dbg == f"sg{l}":
                self.dump_fm(gate)
                return True
            wb, _ = self.load_weight([self.dram["aWo"][l, dc * 128:(dc + 1) * 128, :]
                                      for dc in range(NCH)])
            self._proj_accum_into(gate, wb, z, tag=f"wo{l}")
            ar.free(gate)
            if dbg == f"zmid{l}":
                self.dump_fm(z, F32)
                return True
            self._ret_ffn(z, l, dbg)
            if dbg == f"zffn{l}":
                self.dump_fm(z, F32)
                return True
        return False

    def _proj_accum_into(self, src, wbuf, z, tag="pacc"):
        for oc in range(NCH):
            for tc in range(NTC):
                bank = (oc * NTC + tc) % 2
                ps = self.ps.bank(bank)
                pk = Psum.key(bank)
                for dc in range(NCH):
                    self.mm(ps, self.wslice(wbuf, dc, oc * 128, (oc + 1) * 128),
                            self.chunk(src, dc, BF16)[:, tc * TC:(tc + 1) * TC],
                            start=(dc == 0), stop=(dc == NCH - 1),
                            reads=wbuf.ckeys(dc, 1024 * 2) + self.ck(src, dc, BF16),
                            writes=pk, tag=tag + "m")
                zc = self.chunk(z, oc)[:, tc * TC:(tc + 1) * TC]
                self.p.comp("vector",
                            lambda h, o=zc, i=ps: h.scalar_tensor_tensor(
                                out=o, in0=i, scalar=1.0, in1=o,
                                op0=ALU.mult, op1=ALU.add),
                            reads=pk + self.ck(z, oc), writes=self.ck(z, oc),
                            tag=tag + "e")

    def _retention(self, a, l, dbg=None):
        ar = self.ar
        p = self.p
        state = ar.alloc(1024 * 4, "state")
        sap = state.ap(F32, 1024)
        strv = ar.alloc(1024 * 4, "strv")
        strb = ar.alloc(1024 * 2, "strb")
        tbl = [ar.alloc(1024 * 4, f"tbl{i}") for i in range(4)]
        kub = ar.alloc(1024 * 2, "kub")
        vb = ar.alloc(1024 * 2, "vb")
        t1s = ar.alloc(1024 * 4, "t1s")
        t2s = ar.alloc(1024 * 4, "t2s")
        stg = ar.alloc(1024 * 4, "retstg")

        def rot_scale(ps_in, pk, cosb, sinb, outb, tag):
            cos = cosb.ap(F32, 1024)
            sin = sinb.ap(F32, 1024)
            t1 = t1s.ap(F32, 1024)
            t2 = t2s.ap(F32, 1024)
            o = outb.ap(BF16, 1024)
            def v2(x):
                return x.rearrange("p (h two) -> p h two", two=2)
            self.vtt(t1, ps_in, cos, ALU.mult,
                     reads=list(pk) + cosb.keys(), writes=t1s.keys(), tag=tag)
            self.vtt(v2(t2)[:, :, 0], v2(ps_in)[:, :, 1], v2(sin)[:, :, 0],
                     ALU.mult, reads=list(pk) + sinb.keys(), writes=t2s.keys(),
                     tag=tag)
            self.vtt(v2(t2)[:, :, 1], v2(ps_in)[:, :, 0], v2(sin)[:, :, 1],
                     ALU.mult, reads=list(pk) + sinb.keys(), writes=t2s.keys(),
                     tag=tag)
            p.comp("vector",
                   lambda h, o1=v2(o)[:, :, 0], i0=v2(t2)[:, :, 0],
                          i1=v2(t1)[:, :, 0]: h.scalar_tensor_tensor(
                       out=o1, in0=i0, scalar=-1.0, in1=i1,
                       op0=ALU.mult, op1=ALU.add),
                   reads=t1s.keys() + t2s.keys(), writes=outb.keys(), tag=tag)
            self.vtt(v2(o)[:, :, 1], v2(t2)[:, :, 1], v2(t1)[:, :, 1], ALU.add,
                     reads=t1s.keys() + t2s.keys(), writes=outb.keys(), tag=tag)

        def proj_tok(wbuf, j, banks, tag):
            ps = self.ps.bank(banks, 2)
            pk = Psum.key(banks, 2)
            for dc in range(NCH):
                for g in range(2):
                    self.mm(ps[:, g * TC:(g + 1) * TC],
                            self.chunk(a, dc, BF16)[:, j * 128:(j + 1) * 128],
                            self.wslice(wbuf, dc, g * TC, (g + 1) * TC),
                            start=(dc == 0), stop=(dc == NCH - 1),
                            reads=wbuf.ckeys(dc, 1024 * 2) + self.ck(a, dc, BF16),
                            writes=[pk[g]], tag=tag)
            return ps, pk

        def state_upd(kusrc, vsrc, j, tag):
            for h in range(H0):
                ups = self.ps.bank(4 + h % 2)
                upk = Psum.key(4 + h % 2)
                self.mm(ups[:, 0:128],
                        kusrc.ap(BF16, 1024)[:, h * 128:(h + 1) * 128],
                        vsrc.ap(BF16, 1024)[:, h * 128:(h + 1) * 128],
                        start=True, stop=True,
                        reads=kusrc.keys() + vsrc.keys(), writes=upk, tag=tag)
                ssl = sap[:, h * 128:(h + 1) * 128]
                self.vtt(ssl, ssl, ups[:, 0:128], ALU.add,
                         reads=state.keys() + upk, writes=state.keys(), tag=tag)
                self.vts(ssl, ssl, self.col_ap("lamC", h), None, ALU.mult,
                         ALU.bypass, reads=state.keys() + self.k_vecs,
                         writes=state.keys(), tag=tag)

        wk, _ = self.load_weight([self.dram["aWkT"][l, dc * 128:(dc + 1) * 128, :]
                                  for dc in range(NCH)], tag="ldwk")
        wv, _ = self.load_weight([self.dram["aWvT"][l, dc * 128:(dc + 1) * 128, :]
                                  for dc in range(NCH)], tag="ldwv")
        p.comp("vector", lambda h: h.memset(sap, 0.0), writes=state.keys(),
               tag="stz")
        for j in range(NRCH):
            cb, sb = tbl[2 * (j % 2)], tbl[2 * (j % 2) + 1]
            self.ld(cb.ap(F32, 1024), self.dram["cosk"][j], writes=cb.keys(),
                    tag="tbl")
            self.ld(sb.ap(F32, 1024), self.dram["sink"][j], writes=sb.keys(),
                    tag="tbl")
            kps, kpk = proj_tok(wk, j, 0, "kproj")
            vps, vpk = proj_tok(wv, j, 2, "vproj")
            rot_scale(kps, kpk, cb, sb, kub, "krot")
            self.scopy(vb.ap(BF16, 1024), vps, reads=vpk, writes=vb.keys(),
                       tag="vevac")
            self.st(self.dram["ku_d"][j * 128:(j + 1) * 128, :], kub.ap(BF16, 1024),
                    reads=kub.keys(), writes=[f"ku{l}/{j}"], tag="kust")
            self.st(self.dram["v_d"][j * 128:(j + 1) * 128, :], vb.ap(BF16, 1024),
                    reads=vb.keys(), writes=[f"v{l}/{j}"], tag="vst")
            state_upd(kub, vb, j, "stupA")
        self.vts(stg.ap(F32, 1024), sap, self.col_ap("statemask"), None,
                 ALU.mult, ALU.bypass, reads=state.keys() + self.k_vecs,
                 writes=stg.keys(), tag="stmask")
        self.st(self.dram["st_in"][:, :], stg.ap(F32, 1024), reads=stg.keys(),
                writes=["st_in"], tag="stst")
        p.cc(lambda h: h.collective_compute(
                 "AllReduce", ALU.add,
                 replica_groups=[[0, 1], [2, 3], [4, 5], [6, 7]],
                 ins=[self.dram["st_in"][:, :]], outs=[self.dram["st_out"][:, :]]),
             reads=["st_in"], writes=["st_out"], tag="star")
        self.ld(strv.ap(F32, 1024), self.dram["st_out"][:, :], reads=["st_out"],
                writes=strv.keys(), tag="strld")
        self.vcopy(strb.ap(BF16, 1024), strv.ap(F32, 1024), reads=strv.keys(),
                   writes=strb.keys(), tag="strc")

        wq, _ = self.load_weight([self.dram["aWqT"][l, dc * 128:(dc + 1) * 128, :]
                                  for dc in range(NCH)], tag="ldwq")
        p.comp("vector", lambda h: h.memset(sap, 0.0), writes=state.keys(),
               tag="stz")
        qdb = kub
        qdT = ar.alloc(128 * 2, "qdT")
        kuT = ar.alloc(128 * 2, "kuT")
        qdcT = ar.alloc(128 * 2, "qdcT")
        smb = ar.alloc(128 * 2, "smb")
        stateb = ar.alloc(128 * 2, "stateb")
        hst = ar.alloc(128 * 2, "hst")
        kuld = ar.alloc(1024 * 2, "kuld")
        vld = ar.alloc(1024 * 2, "vld")
        tril = self.sb_tril.ap(BF16, 128)
        for j in range(NRCH):
            cb, sb = tbl[2 * (j % 2)], tbl[2 * (j % 2) + 1]
            self.ld(cb.ap(F32, 1024), self.dram["cosq"][j], writes=cb.keys(),
                    tag="tbl")
            self.ld(sb.ap(F32, 1024), self.dram["sinq"][j], writes=sb.keys(),
                    tag="tbl")
            self.ld(kuld.ap(BF16, 1024), self.dram["ku_d"][j * 128:(j + 1) * 128, :],
                    reads=[f"ku{l}/{j}"], writes=kuld.keys(), tag="kuld")
            self.ld(vld.ap(BF16, 1024), self.dram["v_d"][j * 128:(j + 1) * 128, :],
                    reads=[f"v{l}/{j}"], writes=vld.keys(), tag="vld")
            qps, qpk = proj_tok(wq, j, 0, "qproj")
            rot_scale(qps, qpk, cb, sb, qdb, "qrot")
            for h in range(H0):
                hsl = slice(h * 128, (h + 1) * 128)
                tp = self.ps.bank(2).bitcast(BF16)[:, 0:128]
                tk = Psum.key(2)
                self.transp(tp, qdb.ap(BF16, 1024)[:, hsl],
                            reads=qdb.keys(), writes=tk, tag="trq")
                self.scopy(qdT.ap(BF16, 128), tp, reads=tk,
                           writes=qdT.keys(), tag="trqe")
                tp2 = self.ps.bank(3).bitcast(BF16)[:, 0:128]
                tk2 = Psum.key(3)
                self.transp(tp2, kuld.ap(BF16, 1024)[:, hsl],
                            reads=kuld.keys(), writes=tk2, tag="trk")
                self.scopy(kuT.ap(BF16, 128), tp2, reads=tk2,
                           writes=kuT.keys(), tag="trke")
                sps = self.ps.bank(6)
                spk = Psum.key(6)
                self.mm(sps[:, 0:128], kuT.ap(BF16, 128), qdT.ap(BF16, 128),
                        start=True, stop=True, reads=kuT.keys() + qdT.keys(),
                        writes=spk, tag="smm")
                self.vtt(smb.ap(BF16, 128), sps[:, 0:128], tril, ALU.mult,
                        reads=spk + self.k_tril, writes=smb.keys(), tag="smask")
                ops_ = self.ps.bank(7)
                opk = Psum.key(7)
                self.mm(ops_[:, 0:128], vld.ap(BF16, 1024)[:, hsl],
                        smb.ap(BF16, 128), start=True, stop=False,
                        reads=vld.keys() + smb.keys(), writes=opk, tag="intra")
                self.vcopy(stateb.ap(BF16, 128), sap[:, hsl],
                           reads=state.keys(), writes=stateb.keys(), tag="stc")
                self.mm(ops_[:, 0:128], stateb.ap(BF16, 128), qdT.ap(BF16, 128),
                        start=False, stop=False,
                        reads=stateb.keys() + qdT.keys(), writes=opk, tag="inter")
                self.vts(qdcT.ap(BF16, 128), qdT.ap(BF16, 128),
                         self.col_ap("corr", j * H0 + h), None, ALU.mult,
                         ALU.bypass, reads=qdT.keys() + self.k_vecs,
                         writes=qdcT.keys(), tag="corrs")
                self.mm(ops_[:, 0:128], strb.ap(BF16, 1024)[:, hsl],
                        qdcT.ap(BF16, 128), start=False, stop=True,
                        reads=strb.keys() + qdcT.keys(), writes=opk, tag="corr")
                self.scopy(hst.ap(BF16, 128), ops_[:, 0:128], reads=opk,
                           writes=hst.keys(), tag="oevac")
                self.st(self.dram["heads_d"][h * 128:(h + 1) * 128,
                                             j * 128:(j + 1) * 128],
                        hst.ap(BF16, 128), reads=hst.keys(),
                        writes=[f"hd{l}/{j}/{h}"], tag="hdst")
                ups = self.ps.bank(4 + h % 2)
                upk = Psum.key(4 + h % 2)
                self.mm(ups[:, 0:128], kuld.ap(BF16, 1024)[:, hsl],
                        vld.ap(BF16, 1024)[:, hsl], start=True, stop=True,
                        reads=kuld.keys() + vld.keys(), writes=upk, tag="stupB")
                ssl = sap[:, hsl]
                self.vtt(ssl, ssl, ups[:, 0:128], ALU.add,
                         reads=state.keys() + upk, writes=state.keys(),
                         tag="stupB")
                self.vts(ssl, ssl, self.col_ap("lamC", h), None, ALU.mult,
                         ALU.bypass, reads=state.keys() + self.k_vecs,
                         writes=state.keys(), tag="stupB")
        for b in (state, strv, strb, kub, vb, t1s, t2s, stg, qdT, kuT, qdcT,
                  smb, stateb, hst, kuld, vld, *tbl):
            ar.free(b)

    def _ret_ffn(self, z, l, dbg):
        ar = self.ar
        xn2 = ar.alloc(NCH * T * 2, "xn2")
        self.ln_fm(z, xn2, EPS_A,
                   gcol=lambda c: self.col_ap(f"a_ln2_s{l}", c),
                   bcol=lambda c: self.col_ap(f"a_ln2_b{l}", c), tag=f"ln2_{l}")
        hid = ar.alloc(32 * 1024 * 2, "ffnhid")

        def hap(hc):
            return hid.ap(BF16, 32 * 1024)[:, hc * 1024:(hc + 1) * 1024]

        for half in range(2):
            t0 = half * 1024
            for p in range(4):
                wb, _ = self.load_weight(
                    [self.dram["aW1"][l, dc * 128:(dc + 1) * 128,
                                      p * 1024:(p + 1) * 1024]
                     for dc in range(NCH)], tag="ldw1")
                for sub in range(NCH):
                    hc = p * 8 + sub
                    ps = self.ps.bank(0 if hc % 2 == 0 else 2, 2)
                    pk = Psum.key(0 if hc % 2 == 0 else 2, 2)
                    for dc in range(NCH):
                        for g in range(2):
                            self.mm(ps[:, g * TC:(g + 1) * TC],
                                    self.wslice(wb, dc, sub * 128, (sub + 1) * 128),
                                    self.chunk(xn2, dc, BF16)[:, t0 + g * TC:
                                                              t0 + (g + 1) * TC],
                                    start=(dc == 0), stop=(dc == NCH - 1),
                                    reads=wb.ckeys(dc, 1024 * 2) + self.ck(xn2, dc, BF16),
                                    writes=[pk[g]], tag="ffn1m")
                    self.act(hap(hc), ps, AF.Gelu_apprx_tanh, reads=pk,
                             writes=hid.ckeys(hc, 1024 * 2), tag="ffn1e")
            for ocg in range(2):
                ocs = [ocg * 4 + i for i in range(4)]
                for p in range(4):
                    wb2, _ = self.load_weight(
                        [self.dram["aW2"][l, p * 1024 + dc * 128:
                                          p * 1024 + (dc + 1) * 128, :]
                         for dc in range(NCH)], tag="ldw2")
                    for dc in range(NCH):
                        hc = p * 8 + dc
                        for i, oc in enumerate(ocs):
                            ps = self.ps.bank(2 * i, 2)
                            pk = Psum.key(2 * i, 2)
                            for g in range(2):
                                self.mm(ps[:, g * TC:(g + 1) * TC],
                                        self.wslice(wb2, dc, oc * 128, (oc + 1) * 128),
                                        hap(hc)[:, g * TC:(g + 1) * TC],
                                        start=(p == 0 and dc == 0),
                                        stop=(p == 3 and dc == NCH - 1),
                                        reads=wb2.ckeys(dc, 1024 * 2)
                                              + hid.ckeys(hc, 1024 * 2),
                                        writes=[pk[g]], tag="ffn2m")
                for i, oc in enumerate(ocs):
                    ps = self.ps.bank(2 * i, 2)
                    pk = Psum.key(2 * i, 2)
                    for g in range(2):
                        zc = self.chunk(z, oc)[:, t0 + g * TC:t0 + (g + 1) * TC]
                        self.p.comp("vector",
                                    lambda h, o=zc, i2=ps[:, g * TC:(g + 1) * TC]:
                                        h.scalar_tensor_tensor(
                                            out=o, in0=i2, scalar=1.0, in1=o,
                                            op0=ALU.mult, op1=ALU.add),
                                    reads=pk + self.ck(z, oc),
                                    writes=self.ck(z, oc), tag="ffn2e")
        ar.free(hid)
        ar.free(xn2)

    def load_weight_cols(self, dram2d, col0, dtype=F32, tag="ldwc"):
        buf = self.wbf[self.wpar]
        self.wpar ^= 1
        Dtot = dram2d.shape[-1]
        for dc in range(NCH):
            rows = dram2d[dc * 128:(dc + 1) * 128, :]
            if col0 + 1024 <= Dtot:
                dst = buf.ap(BF16, NCH * 1024)[:, dc * 1024:(dc + 1) * 1024]
                self.p.dma("gpsimd",
                           lambda h, d=dst, s=rows[:, col0:col0 + 1024]:
                               h.dma_start(out=d, in_=s),
                           writes=buf.ckeys(dc, 1024 * 2), tag=tag)
            else:
                n1 = Dtot - col0
                d1 = buf.ap(BF16, NCH * 1024)[:, dc * 1024:dc * 1024 + n1]
                d2 = buf.ap(BF16, NCH * 1024)[:, dc * 1024 + n1:(dc + 1) * 1024]
                self.p.dma("gpsimd",
                           lambda h, d=d1, s=rows[:, col0:Dtot]:
                               h.dma_start(out=d, in_=s),
                           writes=buf.ckeys(dc, 1024 * 2), tag=tag)
                self.p.dma("gpsimd",
                           lambda h, d=d2, s=rows[:, 0:1024 - n1]:
                               h.dma_start(out=d, in_=s),
                           writes=buf.ckeys(dc, 1024 * 2), tag=tag)
        return buf

    def l1_attn(self, xn1, s1, dbg=None):
        ar = self.ar
        p = self.p
        sc = float(1.0 / math.sqrt(D1))
        Q = ar.alloc(NCH * T * 2, "l1Q")
        Qs = ar.alloc(NCH * T * 2, "l1Qs")
        Kf = ar.alloc(NCH * T * 2, "l1K")
        wb, _ = self.load_weight([self.dram["l1wq"][dc * 128:(dc + 1) * 128, :]
                                  for dc in range(NCH)])
        self.proj_fm(xn1, wb, Q, func=AF.Identity,
                     bias=lambda oc: self.col_ap("l1_bq", oc), tag="l1q")
        wb = self.load_weight_cols(self.dram["l1wq"], 64)
        self.proj_fm(xn1, wb, Qs, func=AF.Identity,
                     bias=lambda oc: self.col_ap("l1_bq_sh", oc), tag="l1qs")
        wb, _ = self.load_weight([self.dram["l1wk"][dc * 128:(dc + 1) * 128, :]
                                  for dc in range(NCH)])
        self.proj_fm(xn1, wb, Kf, func=AF.Identity,
                     bias=lambda oc: self.col_ap("l1_bk", oc), tag="l1k")
        wv, _ = self.load_weight([self.dram["l1wv"][dc * 128:(dc + 1) * 128, :]
                                  for dc in range(NCH)])
        Vx = ar.alloc(2 * 1040 * 2, "l1Vx")
        pt = ar.alloc(2 * 256 * 2, "l1pt")
        rrow = ar.alloc(256 * 4, "l1rr")
        rbc = ar.alloc(256 * 4, "l1rbc")
        ctxf32 = ar.alloc(16 * 256 * 4, "l1cf")
        self.ctxstg = ar.alloc(256 * 2, "l1cs")
        jsel = self.sb_jsel.ap(F32, 1024)
        jhot = self.sb_jhot.ap(F32, 256)

        def vx_ap(tile):
            return Vx.ap(BF16, 2 * 1040)[:, tile * 1040:(tile + 1) * 1040]

        p.comp("vector",
               lambda h, o=Vx.ap(BF16, 2 * 1040).rearrange(
                   "p (t j e) -> p t j e", t=2, e=65)[:, :, :, 64:65]:
                   h.memset(o, 1.0),
               writes=Vx.keys(), tag="l1vone")

        def qblock(j, op):
            if 64 * (j % 2) == op:
                cc, qoff, buf = j // 2, op, Q
            elif op == 0:
                cc, qoff, buf = (j - 1) // 2, 0, Qs
            else:
                cc, qoff, buf = ((j - 2) // 2) % 8, 64, Qs
            apq = self.chunk(buf, cc, BF16)
            return apq[qoff:qoff + 64, :], self.ck(buf, cc, BF16)

        for g in range(NGRP):
            tok0 = g * GRP
            qsl = slice(tok0, tok0 + GRP)
            for tile in range(2):
                ps = self.ps.bank(0, 2)
                pk = Psum.key(0, 2)
                tsl = slice(tok0 + tile * 128, tok0 + (tile + 1) * 128)
                for dc in range(NCH):
                    for gg in range(2):
                        self.mm(ps[:, gg * TC:(gg + 1) * TC],
                                self.chunk(xn1, dc, BF16)[:, tsl],
                                self.wslice(wv, dc, gg * TC, (gg + 1) * TC),
                                start=(dc == 0), stop=(dc == NCH - 1),
                                reads=wv.ckeys(dc, 1024 * 2) + self.ck(xn1, dc, BF16),
                                writes=[pk[gg]], tag="l1vm")
                dst = vx_ap(tile).rearrange("p (j e) -> p j e", e=65)[:, :, 0:64]
                self.scopy(dst, ps.rearrange("p (j e) -> p j e", e=64),
                           reads=pk, writes=Vx.keys(), tag="l1ve")
            lps = self.ps.bank(5)
            lpk = Psum.key(5)
            for j in range(16):
                cps = self.ps.bank(2)
                cpk = Psum.key(2)
                first = True
                for jp in range(16):
                    op = 64 * (jp % 2)
                    qap, qk = qblock(j, op)
                    kap = self.chunk(Kf, jp // 2, BF16)
                    kb = self.ck(Kf, jp // 2, BF16)
                    for tile in range(2):
                        sps = self.ps.bank(3 + tile % 2)
                        spk = Psum.key(3 + tile % 2)
                        tsl = slice(tok0 + tile * 128, tok0 + (tile + 1) * 128)
                        self.mm(sps[0:128, 0:GRP], kap[op:op + 64, tsl],
                                qap[:, qsl], start=True, stop=True,
                                reads=kb + qk, writes=spk, tag="l1s")
                        ptap = pt.ap(BF16, 512)[:, tile * 256:(tile + 1) * 256]
                        self.act(ptap, sps[0:128, 0:GRP], AF.Exp, reads=spk,
                                 writes=pt.keys(), scale=sc, tag="l1x")
                        self.mm(cps[0:65, 0:GRP],
                                vx_ap(tile)[:, jp * 65:(jp + 1) * 65],
                                ptap, start=first, stop=(jp == 15 and tile == 1),
                                reads=Vx.keys() + pt.keys(), writes=cpk,
                                tag="l1pv")
                        first = False
                lrow64 = self.lrow64.ap(F32, 256, parts=1, p0=64)
                self.scopy(lrow64, cps[64:65, 0:GRP], reads=cpk,
                           writes=self.lrow64.keys(), tag="l1le")
                self.mm(lps[0:16, 0:GRP], jhot[64:65, j * 16:(j + 1) * 16],
                        lrow64, start=(j == 0), stop=(j == 15),
                        reads=self.lrow64.keys() + self.k_jhot, writes=lpk,
                        tag="l1lc")
                self.scopy(ctxf32.ap(F32, 16 * 256, parts=64)[:, j * 256:(j + 1) * 256],
                           cps[0:64, 0:GRP], reads=cpk,
                           writes=ctxf32.keys(), tag="l1ce")
            self.p.comp("vector",
                        lambda h, o=rrow.ap(F32, 256, parts=16),
                               i=lps[0:16, 0:GRP]:
                            h.reciprocal(out=o, in_=i),
                        reads=lpk, writes=rrow.keys(), tag="l1r")
            for j in range(16):
                bps = self.ps.bank(6)
                bpk = Psum.key(6)
                self.mm(bps[0:64, 0:GRP], jsel[0:16, j * 64:(j + 1) * 64],
                        rrow.ap(F32, 256, parts=16), start=True, stop=True,
                        reads=rrow.keys() + self.k_jsel, writes=bpk, tag="l1rb")
                self.scopy(rbc.ap(F32, 256, parts=64), bps[0:64, 0:GRP],
                           reads=bpk, writes=rbc.keys(), tag="l1rbe")
                fc, po = j // 2, 64 * (j % 2)
                csl = self.ctxstg.ap(BF16, 256, parts=64)
                self.vtt(csl,
                         ctxf32.ap(F32, 16 * 256, parts=64)[:, j * 256:(j + 1) * 256],
                         rbc.ap(F32, 256, parts=64), ALU.mult,
                         reads=ctxf32.keys() + rbc.keys(),
                         writes=self.ctxstg.keys(), tag="l1dv")
                bvcol = self.col_ap("l1_bv", fc)[po:po + 64, :]
                self.vts(csl, csl, bvcol, None, ALU.add, ALU.bypass,
                         reads=self.ctxstg.keys() + self.k_vecs,
                         writes=self.ctxstg.keys(), tag="l1bv")
                self.st(self.dram["ctx_d"][j * 64:(j + 1) * 64, tok0:tok0 + GRP],
                        csl, reads=self.ctxstg.keys(),
                        writes=[f"ctxd/{g}/{j}"], tag="l1cst")
        for b in (Q, Qs, Kf, Vx, pt, rrow, rbc, ctxf32, self.ctxstg):
            ar.free(b)
        ctx = ar.alloc(NCH * T * 2, "l1ctx")
        ctxkeys = [f"ctxd/{gg}/{jj}" for gg in range(NGRP) for jj in range(16)]
        for c in range(NCH):
            self.ld(self.chunk(ctx, c, BF16),
                    self.dram["ctx_d"][c * 128:(c + 1) * 128, :],
                    reads=ctxkeys, writes=self.ck(ctx, c, BF16), tag="l1cr")
        s1raw = ar.alloc(NCH * T * 2, "s1raw")
        wb, _ = self.load_weight([self.dram["l1wo"][dc * 128:(dc + 1) * 128, :]
                                  for dc in range(NCH)])
        self.proj_fm(ctx, wb, s1raw, func=AF.Identity,
                     bias=lambda oc: self.col_ap("l1_bo", oc), tag="l1o")
        ar.free(ctx)
        s1 = ar.alloc(NCH * T * 2, "s1")
        self._s1buf = s1
        for c in range(NCH):
            sc2 = self.chunk(s1raw, c, BF16)
            self.vtt(sc2, sc2, self.chunk(xn1, c, BF16), ALU.add,
                     reads=self.ck(s1raw, c, BF16) + self.ck(xn1, c, BF16),
                     writes=self.ck(s1raw, c, BF16), tag="l1res")
        self.ln_fm(s1raw, s1, EPS_1,
                   gcol=lambda c: self.col_ap("l1_gamma", c),
                   bcol=lambda c: self.col_ap("l1_beta", c), src_dtype=BF16,
                   tag="l1ln")
        ar.free(s1raw)

    def l1_ffn(self, xn1, g1, dbg=None):
        ar = self.ar
        hid = ar.alloc(24 * 1024 * 2, "l1hid")
        gtmp = ar.alloc(1024 * 2, "l1gt")
        utmp = ar.alloc(1024 * 2, "l1ut")

        def hap(hc):
            return hid.ap(BF16, 24 * 1024)[:, hc * 1024:(hc + 1) * 1024]

        for half in range(2):
            t0 = half * 1024
            for p in range(3):
                wg, _ = self.load_weight(
                    [self.dram["l1wgT"][dc * 128:(dc + 1) * 128,
                                        p * 1024:(p + 1) * 1024]
                     for dc in range(NCH)], tag="ldwg")
                wu, _ = self.load_weight(
                    [self.dram["l1wuT"][dc * 128:(dc + 1) * 128,
                                        p * 1024:(p + 1) * 1024]
                     for dc in range(NCH)], tag="ldwu")
                for sub in range(NCH):
                    hc = p * 8 + sub
                    gps = self.ps.bank(0, 2)
                    gpk = Psum.key(0, 2)
                    ups = self.ps.bank(2, 2)
                    upk = Psum.key(2, 2)
                    for dc in range(NCH):
                        for gg in range(2):
                            self.mm(gps[:, gg * TC:(gg + 1) * TC],
                                    self.wslice(wg, dc, sub * 128, (sub + 1) * 128),
                                    self.chunk(xn1, dc, BF16)[:, t0 + gg * TC:
                                                              t0 + (gg + 1) * TC],
                                    start=(dc == 0), stop=(dc == NCH - 1),
                                    reads=wg.ckeys(dc, 1024 * 2) + self.ck(xn1, dc, BF16),
                                    writes=[gpk[gg]], tag="l1f1m")
                            self.mm(ups[:, gg * TC:(gg + 1) * TC],
                                    self.wslice(wu, dc, sub * 128, (sub + 1) * 128),
                                    self.chunk(xn1, dc, BF16)[:, t0 + gg * TC:
                                                              t0 + (gg + 1) * TC],
                                    start=(dc == 0), stop=(dc == NCH - 1),
                                    reads=wu.ckeys(dc, 1024 * 2) + self.ck(xn1, dc, BF16),
                                    writes=[upk[gg]], tag="l1f1m")
                    self.act(gtmp.ap(BF16, 1024), gps, AF.Silu, reads=gpk,
                             writes=gtmp.keys(), scale=QSCALE, tag="l1f1e")
                    self.act(utmp.ap(BF16, 1024), ups, AF.Identity, reads=upk,
                             writes=utmp.keys(), scale=QSCALE, tag="l1f1e")
                    self.vtt(hap(hc), gtmp.ap(BF16, 1024), utmp.ap(BF16, 1024),
                             ALU.mult, reads=gtmp.keys() + utmp.keys(),
                             writes=hid.ckeys(hc, 1024 * 2), tag="l1f1h")
            for ocg in range(2):
                ocs = [ocg * 4 + i for i in range(4)]
                for p in range(3):
                    wd, _ = self.load_weight(
                        [self.dram["l1wdT"][p * 1024 + dc * 128:
                                            p * 1024 + (dc + 1) * 128, :]
                         for dc in range(NCH)], tag="ldwd")
                    for dc in range(NCH):
                        hc = p * 8 + dc
                        for i, oc in enumerate(ocs):
                            ps = self.ps.bank(2 * i, 2)
                            pk = Psum.key(2 * i, 2)
                            for gg in range(2):
                                self.mm(ps[:, gg * TC:(gg + 1) * TC],
                                        self.wslice(wd, dc, oc * 128, (oc + 1) * 128),
                                        hap(hc)[:, gg * TC:(gg + 1) * TC],
                                        start=(p == 0 and dc == 0),
                                        stop=(p == 2 and dc == NCH - 1),
                                        reads=wd.ckeys(dc, 1024 * 2)
                                              + hid.ckeys(hc, 1024 * 2),
                                        writes=[pk[gg]], tag="l1f2m")
                for i, oc in enumerate(ocs):
                    ps = self.ps.bank(2 * i, 2)
                    pk = Psum.key(2 * i, 2)
                    self.act(self.chunk(g1, oc, BF16)[:, t0:t0 + 1024], ps,
                             AF.Silu, reads=pk, writes=self.ck(g1, oc, BF16),
                             scale=QSCALE, tag="l1f2e")
        ar.free(hid)
        ar.free(gtmp)
        ar.free(utmp)

    def cast_fm(self, src, dst, tag="cast"):
        for c in range(NCH):
            self.vcopy(self.chunk(dst, c, BF16), self.chunk(src, c),
                       reads=self.ck(src, c), writes=self.ck(dst, c, BF16),
                       tag=tag)

    def store_fm(self, src, dram_name, dtype=F32, tag="stout"):
        for c in range(NCH):
            self.st(self.dram[dram_name][c * 128:(c + 1) * 128, :],
                    self.chunk(src, c, dtype),
                    reads=self.ck(src, c, dtype),
                    writes=[f"{dram_name}/{c}"], tag=tag)

    def dump_fm(self, buf, dtype=BF16, tag="dump"):
        for c in range(NCH):
            self.st(self.dram["out"][c * 128:(c + 1) * 128, :],
                    self.chunk(buf, c, dtype), reads=self.ck(buf, c, dtype),
                    writes=[f"out/{c}"], tag=tag)
        self.fence([f"out/{c}" for c in range(NCH)])

    def fence(self, keys):
        self.p.comp("gpsimd", lambda h: h.nop(), reads=list(keys), tag="fence")



NVCOLS = 512

ROW_SPEC = [("r_ones", 1), ("l1sel", 2), ("r_zeros", 1)]
ROW_FIXED = {"r_ones": 0, "l1sel": 1, "r_zeros": 3, "r_ones32": 32, "r_ones64": 64}

COL_SPEC = (
    [(f"a_ln1_s{l}", 8) for l in range(2)] + [(f"a_ln1_b{l}", 8) for l in range(2)]
    + [(f"a_ln2_s{l}", 8) for l in range(2)] + [(f"a_ln2_b{l}", 8) for l in range(2)]
    + [(f"a_gn_s{l}", 8) for l in range(2)] + [(f"a_gn_b{l}", 8) for l in range(2)]
    + [("f_ln_g", 8), ("f_ln_b", 8)]
    + [("l1_gamma", 8), ("l1_beta", 8)]
    + [("l1_bq", 8), ("l1_bk", 8), ("l1_bv", 8), ("l1_bo", 8), ("l1_bq_sh", 8)]
    + [(f"f_be{e}", 8) for e in range(20)]
    + [("ew", 18), ("act_w", 6), ("lamC", 8), ("corr", NRCH * H0),
       ("statemask", 1), ("c_zeros", 1), ("c_ones", 1)]
)


def _spec_index(spec):
    out = {}
    i = 0
    for name, n in spec:
        out[name] = (i, n)
        i += n
    return out, i


ROW_IDX = {"r_ones": (0, 1), "l1sel": (1, 2), "r_zeros": (3, 1),
           "r_ones32": (32, 1), "r_ones64": (64, 1)}
COL_IDX, NCOLS_USED = _spec_index(COL_SPEC)
assert NCOLS_USED <= NVCOLS


def pack_consts(inputs, halfidx, core_tabs):
    rows = np.zeros((128, 128), np.float32)
    rows[0, :] = 1.0
    rows[32, :] = 1.0
    rows[64, :] = 1.0
    rows[1, 0:64] = 1.0
    rows[2, 64:128] = 1.0

    vecs = np.zeros((128, NVCOLS), np.float32)

    def put_vec(name, v):
        i, n = COL_IDX[name]
        v = np.asarray(v, np.float32).reshape(-1)
        assert v.size == n * 128
        vecs[:, i:i + n] = v.reshape(n, 128).T

    def put_cols(name, arr):
        i, n = COL_IDX[name]
        arr = np.asarray(arr, np.float32)
        assert arr.shape == (128, n), (name, arr.shape)
        vecs[:, i:i + n] = arr

    for l in range(2):
        put_vec(f"a_ln1_s{l}", inputs["L0_attn_ln1_scale"][l])
        put_vec(f"a_ln1_b{l}", inputs["L0_attn_ln1_bias"][l])
        put_vec(f"a_ln2_s{l}", inputs["L0_attn_ln2_scale"][l])
        put_vec(f"a_ln2_b{l}", inputs["L0_attn_ln2_bias"][l])
        put_vec(f"a_gn_s{l}", inputs["L0_attn_gn_scale"][l])
        put_vec(f"a_gn_b{l}", inputs["L0_attn_gn_bias"][l])
    put_vec("f_ln_g", inputs["L0_ffn_ln_g"])
    put_vec("f_ln_b", inputs["L0_ffn_ln_b"])
    put_vec("l1_gamma", inputs["L1_attn_gamma"])
    put_vec("l1_beta", inputs["L1_attn_beta"])
    put_vec("l1_bq", inputs["L1_attn_bq"])
    put_vec("l1_bq_sh", np.roll(np.asarray(inputs["L1_attn_bq"], np.float32), -64))
    put_vec("l1_bk", inputs["L1_attn_bk"])
    put_vec("l1_bv", inputs["L1_attn_bv"])
    put_vec("l1_bo", inputs["L1_attn_bo"])
    for e in range(20):
        put_vec(f"f_be{e}", inputs["L0_ffn_be"][e])
    ew = np.asarray(inputs["L0_ffn_ew"], np.float32).reshape(-1)
    put_cols("ew", np.broadcast_to(ew[None, :], (128, 18)))
    aw = np.asarray(inputs["L0_ffn_act_w"], np.float32).reshape(-1)
    put_cols("act_w", np.broadcast_to(aw[None, :], (128, 6)))
    put_cols("lamC", core_tabs["lamC"])
    put_cols("corr", core_tabs["corr"])
    put_cols("statemask", core_tabs["statemask"])
    i, _ = COL_IDX["c_ones"]
    vecs[:, i] = 1.0
    return rows, vecs



def build_program(dbg=None):
    nc = bass.Bass("TRN2", target_bir_lowering=False, debug=False,
                   detect_race_conditions=False)
    stack = contextlib.ExitStack()
    k = K(nc, stack, dbg)
    k._stack = stack
    k.rows = dict(ROW_IDX)
    k.cols = {n: COL_IDX[n] for n, _ in COL_SPEC}
    k._nvcols = NVCOLS

    k.din("xT", [D, T])
    k.din("rows", [128, 128])
    k.din("vecs", [128, NVCOLS])
    k.din("ident", [128, 128])
    k.din("trilmask", [128, 128])
    k.din("cosq", [NRCH, CH, D]); k.din("sinq", [NRCH, CH, D])
    k.din("cosk", [NRCH, CH, D]); k.din("sink", [NRCH, CH, D])
    k.din("aWq", [2, H0, D, E0]); k.din("aWk", [2, H0, D, E0]); k.din("aWv", [2, H0, D, E0])
    k.din("aWg", [2, D, D]); k.din("aWo", [2, D, D])
    k.din("aW1", [2, D, FF0]); k.din("aW2", [2, FF0, D])
    k.din("fWe", [20, D, D])
    k.din("fWq", [6, D, D]); k.din("fWk", [6, D, D]); k.din("fWv", [6, D, D]); k.din("fWo", [6, D, D])
    k.din("l1wq", [D, D]); k.din("l1wk", [D, D]); k.din("l1wv", [D, D]); k.din("l1wo", [D, D])
    k.din("l1wgT", [D, FF1], I32); k.din("l1wuT", [D, FF1], I32); k.din("l1wdT", [FF1, D], I32)
    k.din("cvals", [128, 64])
    k.din("hsel", [8, 1024])
    k.din("jsel", [16, 1024])
    k.din("jhot", [128, 256])
    k.dtmp("ndbf", [7, D, T])
    k.dtmp("n5f", [D, T])
    k.dtmp("xn0f_d", [D, T])
    k.dtmp("g0_d", [D, T], BF16)
    k.dtmp("x1_d", [D, T])
    k.dtmp("ag_in", [T, T], BF16)
    k.dtmp("ag_out", [2 * T, T], BF16)
    k.dtmp("st_in", [128, D])
    k.dtmp("st_out", [128, D])
    k.dtmp("ku_d", [T, D], BF16)
    k.dtmp("v_d", [T, D], BF16)
    k.dtmp("heads_d", [D, T], BF16)
    k.dtmp("ctx_d", [D, T], BF16)
    k.din("aWqT", [2, D, D]); k.din("aWkT", [2, D, D]); k.din("aWvT", [2, D, D])
    k.dout("out", [D, T])

    with stack:
        k.load_consts()
        build_phases(k, dbg)
        cv = k.sb_cvals.ap(F32, 64)
        for v, i in k.constvals.items():
            nc.const_aps.register(F32, v, cv[:, i:i + 1])
        k.p.materialize(nc, stack)
    return nc, k


def build_phases(k, dbg):
    ar = k.ar
    x0 = ar.alloc(NCH * T * 4, "x0")
    k.load_x(x0, "xT")
    xn0_f = ar.alloc(NCH * T * 4, "xn0f")
    k.ln_fm(x0, None, EPS_OUT, nrm_f32=xn0_f, tag="ln0")
    ar.free(x0)
    xn0 = ar.alloc(NCH * T * 2, "xn0")
    k.cast_fm(xn0_f, xn0)
    if dbg == "xn0":
        k.store_fm(xn0_f, "out")
        k.fence([f"out/{c}" for c in range(NCH)])
        return
    for c in range(NCH):
        k.st(k.dram["ndbf"][0, c * 128:(c + 1) * 128, :], k.chunk(xn0_f, c),
             reads=k.ck(xn0_f, c), writes=[f"nd0/{c}"], tag="ndst")
        k.st(k.dram["xn0f_d"][c * 128:(c + 1) * 128, :], k.chunk(xn0_f, c),
             reads=k.ck(xn0_f, c), writes=[f"xn0f_d/{c}"], tag="xnst")
    k.p.comp("gpsimd", lambda h: h.nop(),
             reads=[f"nd0/{c}" for c in range(NCH)], writes=["ndbf_-1"],
             tag="ndfence")
    ar.free(xn0)
    ar.free(xn0_f)
    k.init_wstage()
    stop = k.l0ffn(dbg)
    if stop:
        return
    if dbg == "g0":
        tmp = ar.alloc(NCH * T * 2, "dbgl")
        for cc in range(NCH):
            k.ld(k.chunk(tmp, cc, BF16), k.dram["g0_d"][cc * 128:(cc + 1) * 128, :],
                 reads=[f"g0_d/{c}" for c in range(NCH)],
                 writes=k.ck(tmp, cc, BF16), tag="dbgl")
        k.dump_fm(tmp)
        return
    z = ar.alloc(NCH * T * 4, "z")
    for c in range(NCH):
        k.ld(k.chunk(z, c), k.dram["xn0f_d"][c * 128:(c + 1) * 128, :],
             reads=[f"xn0f_d/{c}"], writes=k.ck(z, c), tag="zld")
    stop = k.l0attn(z, dbg)
    if stop:
        return
    if dbg == "s0":
        k.dump_fm(z, F32)
        return
    g0r = ar.alloc(NCH * T * 2, "g0r")
    for c in range(NCH):
        k.ld(k.chunk(g0r, c, BF16), k.dram["g0_d"][c * 128:(c + 1) * 128, :],
             reads=[f"g0_d/{cc}" for cc in range(NCH)],
             writes=k.ck(g0r, c, BF16), tag="g0r")
    x1 = z
    x0sc = ar.alloc(T * 4, "x0sc")
    for c in range(NCH):
        k.ld(x0sc.ap(F32, T), k.dram["xT"][c * 128:(c + 1) * 128, :],
             writes=x0sc.keys(), tag="x0r")
        k.vtt(k.chunk(x1, c), k.chunk(z, c), k.chunk(g0r, c, BF16), ALU.mult,
              reads=k.ck(z, c) + k.ck(g0r, c, BF16), writes=k.ck(x1, c),
              tag="cmb0")
        k.vtt(k.chunk(x1, c), k.chunk(x1, c), x0sc.ap(F32, T), ALU.add,
              reads=k.ck(x1, c) + x0sc.keys(), writes=k.ck(x1, c), tag="cmb0")
        k.st(k.dram["x1_d"][c * 128:(c + 1) * 128, :], k.chunk(x1, c),
             reads=k.ck(x1, c), writes=[f"x1_d/{c}"], tag="x1st")
    ar.free(g0r)
    ar.free(x0sc)
    if dbg == "x1":
        k.dump_fm(x1, F32)
        return
    xn1 = ar.alloc(NCH * T * 2, "xn1")
    k.ln_fm(x1, xn1, EPS_OUT, tag="ln1o")
    ar.free(x1)
    if dbg == "xn1":
        k.dump_fm(xn1)
        return
    stop = k.l1_attn(xn1, None, dbg)
    s1 = k._s1buf
    if dbg == "s1":
        k.dump_fm(s1)
        return
    g1 = ar.alloc(NCH * T * 2, "g1")
    k.l1_ffn(xn1, g1, dbg)
    ar.free(xn1)
    if dbg == "g1":
        k.dump_fm(g1)
        return
    x1sc = ar.alloc(T * 4, "x1sc")
    osc = ar.alloc(T * 4, "osc")
    for c in range(NCH):
        k.ld(x1sc.ap(F32, T), k.dram["x1_d"][c * 128:(c + 1) * 128, :],
             reads=[f"x1_d/{c}"], writes=x1sc.keys(), tag="x1r")
        k.vtt(osc.ap(F32, T), k.chunk(s1, c, BF16), k.chunk(g1, c, BF16),
              ALU.mult, reads=k.ck(s1, c, BF16) + k.ck(g1, c, BF16),
              writes=osc.keys(), tag="cmb1")
        k.vtt(osc.ap(F32, T), osc.ap(F32, T), x1sc.ap(F32, T), ALU.add,
              reads=osc.keys() + x1sc.keys(), writes=osc.keys(), tag="cmb1")
        k.st(k.dram["out"][c * 128:(c + 1) * 128, :], osc.ap(F32, T),
             reads=osc.keys(), writes=[f"out/{c}"], tag="outst")
    k.fence([f"out/{c}" for c in range(NCH)])


_PROG_CACHE = {}


def _get_program(dbg=None):
    if dbg not in _PROG_CACHE:
        _PROG_CACHE[dbg] = build_program(dbg)
    return _PROG_CACHE[dbg]


def make_in_maps(inputs, k):
    shared, lam2, inv_freq = _host_tables()
    halves = [_core_tables(h, lam2, inv_freq) for h in range(2)]

    f32 = np.float32
    x = np.asarray(inputs["L0_attn_x"], f32)
    wmap = {
        "aWq": np.asarray(inputs["L0_attn_Wq"], f32),
        "aWk": np.asarray(inputs["L0_attn_Wk"], f32),
        "aWv": np.asarray(inputs["L0_attn_Wv"], f32),
        "aWqT": np.ascontiguousarray(np.asarray(inputs["L0_attn_Wq"], f32)
                                     .transpose(0, 2, 1, 3).reshape(2, D, D)),
        "aWkT": np.ascontiguousarray(np.asarray(inputs["L0_attn_Wk"], f32)
                                     .transpose(0, 2, 1, 3).reshape(2, D, D)),
        "aWvT": np.ascontiguousarray(np.asarray(inputs["L0_attn_Wv"], f32)
                                     .transpose(0, 2, 1, 3).reshape(2, D, D)),
        "aWg": np.asarray(inputs["L0_attn_Wg"], f32),
        "aWo": np.asarray(inputs["L0_attn_Wo"], f32),
        "aW1": np.asarray(inputs["L0_attn_ffn_w1"], f32),
        "aW2": np.asarray(inputs["L0_attn_ffn_w2"], f32),
        "fWe": np.asarray(inputs["L0_ffn_We"], f32),
        "fWq": np.asarray(inputs["L0_ffn_Wq"], f32),
        "fWk": np.asarray(inputs["L0_ffn_Wk"], f32),
        "fWv": np.asarray(inputs["L0_ffn_Wv"], f32),
        "fWo": np.asarray(inputs["L0_ffn_Wo"], f32),
        "l1wq": np.asarray(inputs["L1_attn_wq"], f32),
        "l1wk": np.asarray(inputs["L1_attn_wk"], f32),
        "l1wv": np.asarray(inputs["L1_attn_wv"], f32),
        "l1wo": np.asarray(inputs["L1_attn_wo"], f32),
        "l1wgT": np.ascontiguousarray(np.asarray(inputs["L1_ffn_w_gate"], np.int32).T),
        "l1wuT": np.ascontiguousarray(np.asarray(inputs["L1_ffn_w_up"], np.int32).T),
        "l1wdT": np.ascontiguousarray(np.asarray(inputs["L1_ffn_w_down"], np.int32).T),
    }
    in_maps = []
    for core in range(NCORES):
        seq, half = core // 2, core % 2
        xs = np.ascontiguousarray(x[seq, half * T:(half + 1) * T, :].T)
        rows, vecs = pack_consts(inputs, half, {**shared, **halves[half]})
        cvals = np.zeros((128, 64), np.float32)
        for v, i in k.constvals.items():
            cvals[:, i] = v
        hsel = np.zeros((8, 1024), np.float32)
        for hh in range(8):
            hsel[hh, hh * 128:(hh + 1) * 128] = 1.0
        jsel = np.zeros((16, 1024), np.float32)
        for jj in range(16):
            jsel[jj, jj * 64:(jj + 1) * 64] = 1.0
        jhot = np.zeros((128, 256), np.float32)
        for jj in range(16):
            jhot[64, jj * 16 + jj] = 1.0
        m = {
            "xT": xs, "rows": rows, "vecs": vecs, "cvals": cvals, "hsel": hsel,
            "jsel": jsel, "jhot": jhot,
            "ident": shared["ident"], "trilmask": shared["trilmask"],
            "cosq": halves[half]["cosq"], "sinq": halves[half]["sinq"],
            "cosk": halves[half]["cosk"], "sink": halves[half]["sink"],
        }
        m.update(wmap)
        in_maps.append(m)
    return in_maps


def run(inputs, dbg=None):
    nc, k = _get_program(dbg)
    in_maps = make_in_maps(inputs, k)
    res = run_bass_kernel_spmd(nc, in_maps, core_ids=list(range(NCORES)))
    return res


def kernel(**inputs):
    res = run(inputs, None)
    out = np.empty((B, S, D), np.float32)
    for core in range(NCORES):
        seq, half = core // 2, core % 2
        out[seq, half * T:(half + 1) * T, :] = res.results[core]["out"].T
    return out
